# version 12
# baseline (speedup 1.0000x reference)
import os
from contextlib import ExitStack

import numpy as np
import ml_dtypes

import concourse.bass as bass
import concourse.mybir as mybir
from concourse.bass_utils import run_bass_kernel_spmd

F32 = mybir.dt.float32
BF16 = mybir.dt.bfloat16
I32 = mybir.dt.int32
AF = mybir.ActivationFunctionType
ALU = mybir.AluOpType

NCORES = 8




class Cfg:
    def __init__(self, D, H, B, NB, I, LUTN=16):
        self.D, self.H, self.B, self.NB, self.I, self.LUTN = D, H, B, NB, I, LUTN
        self.C = D // 128
        self.IC = I // 128
        self.KH = B // 128
        self.TOK = NB * B
        self.NSEQ = 4 * self.TOK
        self.RB = 3 * NB
        self.TT = min(512, self.TOK)
        assert D == 128 * H and B % 128 == 0 and I % 128 == 0


CFG_FULL = Cfg(D=2048, H=16, B=256, NB=4, I=5504)
CFG_TINY = Cfg(D=256, H=2, B=128, NB=2, I=256)
CFG_MID = Cfg(D=2048, H=16, B=256, NB=1, I=1408)
CFG_MID2 = Cfg(D=2048, H=16, B=128, NB=1, I=256)
CFG_TINY2 = Cfg(D=256, H=2, B=256, NB=1, I=256)


class _DualInst:
    def __init__(self, a, b):
        self._insts = (a, b)

    def then_inc(self, sem, val=1, **kw):
        for i in self._insts:
            i.then_inc(sem, val, **kw)
        return self


class Prog:

    def __init__(self, nc, ctx):
        self.nc = nc
        self._ctx = ctx
        self.sems = {}
        self.count = {}
        self.hwm = {}
        self.last_write = {}
        self.readers = {}
        self.eng = {}
        self._deferred = []
        for name in ("tensor", "vector", "scalar", "gpsimd", "sync"):
            self.sems[name] = ctx.enter_context(nc.semaphore(f"sem_{name}"))
            self.count[name] = 0
            self.eng[name] = getattr(nc, name)

    def _sem(self, name):
        if name not in self.sems:
            self.sems[name] = self._ctx.enter_context(
                self.nc.semaphore(f"sem_{name}"))
            self.count[name] = 0
        return self.sems[name]

    def _wait(self, waiter, eng, sem_name, value):
        if value <= 0:
            return
        key = (waiter, sem_name)
        if self.hwm.get(key, 0) >= value:
            return
        eng.wait_ge(self.sems[sem_name], value)
        self.hwm[key] = value

    def op(self, engine_name, emit_fn, reads=(), writes=(), inc=True,
           sem_name=None, step=1, accum_key=None, accum_continue=False):
        eng = self.eng[engine_name]
        sem_name = sem_name or engine_name
        self._sem(sem_name)
        need = {}

        def add(s, v):
            if need.get(s, 0) < v:
                need[s] = v

        for k in reads:
            lw = self.last_write.get(k)
            if lw:
                add(*lw)
        for k in writes:
            lw = self.last_write.get(k)
            if lw and not (accum_continue and k == accum_key):
                add(*lw)
            for r in self.readers.get(k, ()):
                add(*r)
        for s, v in sorted(need.items()):
            self._wait(engine_name, eng, s, v)
        inst = emit_fn(eng)
        if inc:
            self.count[sem_name] += step
            inst.then_inc(self.sems[sem_name], step)
            mark = (sem_name, self.count[sem_name])
        else:
            mark = (sem_name, self.count[sem_name] + step)
            self._deferred.append(mark)
        for k in reads:
            self.readers.setdefault(k, []).append(mark)
        for k in writes:
            self.last_write[k] = mark
            self.readers[k] = []
        return mark

    def dma(self, queue, emit_fn, reads=(), writes=(), stream="dma"):
        self._sem(stream)
        eng = self.eng[queue]
        self._wait(queue, eng, stream, self.count[stream])
        return self.op(queue, emit_fn, reads=reads, writes=writes, inc=True,
                       sem_name=stream, step=16)

    def wait_all(self, engine_name, sem_names):
        eng = self.eng[engine_name]
        for s in sem_names:
            if s in self.count:
                self._wait(engine_name, eng, s, self.count[s])

    def finalize(self):
        for sem_name, v in self._deferred:
            assert v <= self.count[sem_name], (
                f"inc=False op on {sem_name} not covered "
                f"({v} > {self.count[sem_name]})")


def build(cfg, lut_vals, slope_vals, race_check=False):
    c = cfg
    D, H, B, NB, I = c.D, c.H, c.B, c.NB, c.I
    C, IC, KH, TOK, TT, RB = c.C, c.IC, c.KH, c.TOK, c.TT, c.RB
    NTT = TOK // TT
    TB = TOK // 128
    HGH = min(4, H)
    NHG = H // HGH
    GW = HGH * 128
    SLAB = min(512, D)
    LUTN = c.LUTN
    MHALF = max(I, D) // 2
    lut_f = [float(v) for v in lut_vals]
    slope_f = [float(v) for v in slope_vals]

    nc = bass.Bass("TRN2", target_bir_lowering=False, debug=False,
                   num_devices=NCORES, detect_race_conditions=race_check)

    def din(name, shape, dt):
        return nc.dram_tensor(name, shape, dt, kind="ExternalInput").ap()

    XT = din("XT", [D, c.NSEQ], F32)
    WQKVT = din("WQKVT", [D, 3 * D], F32)
    WGT = din("WGT", [D, D], F32)
    WOT = din("WOT", [D, D], F32)
    NORMWT = din("NORMWT", [128, C], F32)
    SLOPE = din("SLOPE", [1, H], F32)
    CL = C // 2
    ICL = (IC + 1) // 2
    GIDXT = din("GIDXT", [CL * 128, I], I32)
    UIDXT = din("UIDXT", [CL * 128, I], I32)
    DIDXT = din("DIDXT", [ICL * 128, D], I32)
    GATES = din("GATES", [1, max(RB, 2)], F32)
    MEIN = din("MEIN", [1, 2], I32)

    OUTT = nc.dram_tensor("OUTT", [D, TOK], F32, kind="ExternalOutput").ap()

    WGM = nc.dram_tensor("WGM", [D, I], BF16, kind="Internal",
                         addr_space="Shared").ap()
    WUM = nc.dram_tensor("WUM", [D, I], BF16, kind="Internal",
                         addr_space="Shared").ap()
    WDM = nc.dram_tensor("WDM", [2 * ICL * 128, D], BF16, kind="Internal",
                         addr_space="Shared").ap()
    FLG = nc.dram_tensor("FLG", [1, 2 * 6 * 16], I32, kind="Internal",
                         addr_space="Shared").ap()
    XNS = nc.dram_tensor("XNS", [D, TOK], BF16, kind="Internal").ap()
    GATS = nc.dram_tensor("GATS", [D, TOK], BF16, kind="Internal").ap()
    ATTS = nc.dram_tensor("ATTS", [D, TOK], BF16, kind="Internal").ap()
    ACCS = nc.dram_tensor("ACCS", [D, TOK], F32, kind="Internal").ap()

    XTr = XT.rearrange("(c p) n -> p c n", p=128)
    WQKVTr = WQKVT.rearrange("(c p) n -> p c n", p=128)
    WGTr = WGT.rearrange("(c p) n -> p c n", p=128)
    WOTr = WOT.rearrange("(c p) n -> p c n", p=128)
    WGMr = WGM.rearrange("(c p) n -> p c n", p=128)
    WUMr = WUM.rearrange("(c p) n -> p c n", p=128)
    WDMr = WDM.rearrange("(c p) n -> p c n", p=128)
    XNSr = XNS.rearrange("(c p) n -> p c n", p=128)
    ATTSr = ATTS.rearrange("(c p) n -> p c n", p=128)

    with ExitStack() as ctx:
        p = Prog(nc, ctx)

        def sb(name, shape, dt, stk=None):
            return (stk or ctx).enter_context(nc.sbuf_tensor(name, shape, dt))

        def psum(name, shape, dt):
            return ctx.enter_context(nc.psum_tensor(name, shape, dt))

        id128 = nc.inline_tensor(np.eye(128, dtype=ml_dtypes.bfloat16),
                                 "id128").ap()
        ONESK = nc.inline_tensor(np.ones((128, 1), dtype=ml_dtypes.bfloat16),
                                 "onesk").ap()
        ONES1 = nc.inline_tensor(np.ones((1, 128), dtype=np.float32),
                                 "ones1").ap()

        s_id = sb("s_id", [128, 128], BF16)
        s_onesk = sb("s_onesk", [128, 1], BF16)
        s_ones1 = sb("s_ones1", [1, 128], F32)
        s_slope = sb("s_slope", [1, H], F32)
        s_gates = sb("s_gates", [1, max(RB, 2)], F32)
        s_normw = sb("s_normw", [128, C], F32)
        s_rows = sb("s_rows", [1, 3 * TT + max(B, min(TT, 256))], F32)
        s_bdr = sb("s_bdr", [1, H], F32)
        s_bdm1 = sb("s_bdm1", [1, H], F32)
        s_frow = sb("s_frow", [1, H], F32)
        s_bdB = sb("s_bdB", [128, H], F32)
        s_gB = sb("s_gB", [128, max(RB, 2)], F32)
        s_fbc = sb("s_fbc", [128, H], F32)
        s_kdec = sb("s_kdec", [128, KH * H], F32)
        s_gkdec = sb("s_gkdec", [128, KH * H], F32)

        s_w0 = sb("s_w0", [128, C * SLAB], BF16)
        s_w1 = sb("s_w1", [128, C * SLAB], BF16)
        s_acc = sb("s_acc", [128, TT], F32)
        s_sg = sb("s_sg", [128, TT], BF16)
        s_flag = sb("s_flag", [1, 96], I32)
        s_mag = sb("s_mag", [1, 16], I32)
        s_dly = sb("s_dly", [1, 48], I32)
        s_me = sb("s_me", [1, 2], I32)
        s_midx = sb("s_midx", [128, 2 * MHALF], BF16)
        s_macc = sb("s_macc", [128, 2 * MHALF], BF16)
        s_mt0 = sb("s_mt0", [128, MHALF], BF16)
        s_mt1 = sb("s_mt1", [128, MHALF], BF16)

        pb = [psum(f"p{i}", [128, 512], F32) for i in range(6)]
        pT = psum("pT", [128, 256], BF16)

        block = ctx.enter_context(nc.Block())
        V, S_, G, T, SY = "vector", "scalar", "gpsimd", "tensor", "sync"

        cur_keys = {}

        def rekey_regions(bufnm, tb, chunks, new_keys):
            old = []
            for cidx, nk in zip(chunks, new_keys):
                rid = (bufnm, tb, cidx)
                old.extend(cur_keys.get(rid, ()))
                cur_keys[rid] = (nk,)
            return tuple(old)

        def phase_barrier():
            for e in ("tensor", "vector", "scalar", "gpsimd", "sync"):
                p.wait_all(e, list(p.count.keys()))

        for nm, (dram, sbuf) in {
            "id": (id128, s_id), "onesk": (ONESK, s_onesk),
            "ones1": (ONES1, s_ones1), "slope": (SLOPE, s_slope),
            "gates": (GATES, s_gates), "normw": (NORMWT, s_normw),
        }.items():
            p.dma(SY, lambda e, d=dram, s=sbuf: e.dma_start(out=s[:], in_=d),
                  writes=(nm,), stream="sc")

        MAGICI = 0x5A17C0DE
        MAGT = nc.inline_tensor(np.full((1, 16), MAGICI, dtype=np.int32),
                                "magt").ap()
        p.dma(SY, lambda e: e.dma_start(out=s_mag[:], in_=MAGT),
              writes=("mag",), stream="sc")
        p.dma(SY, lambda e: e.dma_start(out=s_me[:], in_=MEIN),
              writes=("mein",), stream="sc")
        p.op(G, lambda e: e.memset(s_dly[:], 1), writes=("dly",))
        p.op(G, lambda e: e.memset(s_flag[:], 0), writes=("flagbuf",))
        rme_g = ctx.enter_context(nc.gpsimd.register("rme_g"))
        rme_s = ctx.enter_context(nc.sync.register("rme_s"))
        p_pr = ctx.enter_context(nc.gpsimd.register("p_pr"))
        p_pz = ctx.enter_context(nc.gpsimd.register("p_pz"))
        p.op(G, lambda e: e.reg_load(rme_g, s_me[0:1, 0:1]),
             reads=("mein",), writes=("rme_g",))
        p.op(SY, lambda e: e.reg_load(rme_s, s_me[0:1, 0:1]),
             reads=("mein",), writes=("rme_s",))

        def zero_flags(e):
            with e.If_eq(rme_s, 0):
                i1 = e.dma_start(out=FLG[0:1, 0:96], in_=s_flag[:])
            with e.Else():
                i2 = e.dma_start(out=FLG[0:1, 96:192], in_=s_flag[:])
            return _DualInst(i1, i2)
        p.dma(SY, zero_flags, reads=("flagbuf", "rme_s"), writes=("flgz",),
              stream="sm")

        p.op(S_, lambda e: e.activation(out=s_bdr[:], in_=s_slope[:],
                                        func=AF.Exp, scale=-float(B)),
             reads=("slope",), writes=("bdr",))
        p.op(V, lambda e: e.tensor_scalar(out=s_bdm1[:], in0=s_bdr[:],
                                          scalar1=-1.0, scalar2=None,
                                          op0=ALU.add, op1=ALU.bypass),
             reads=("bdr",), writes=("bdm1",))
        p.op(T, lambda e: e.matmul(out=pb[4][:, 0:H], lhsT=s_ones1[:],
                                   rhs=s_bdr[:], start=True, stop=True),
             reads=("ones1", "bdr"), writes=("p4",))
        p.op(V, lambda e: e.tensor_copy(out=s_bdB[:], in_=pb[4][:, 0:H]),
             reads=("p4",), writes=("bdB",))
        nrb = s_gates.shape[1]
        p.op(T, lambda e, nrb=nrb: e.matmul(out=pb[4][:, 0:nrb],
                                            lhsT=s_ones1[:], rhs=s_gates[:],
                                            start=True, stop=True),
             reads=("ones1", "gates"), writes=("p4",))
        p.op(V, lambda e, nrb=nrb: e.tensor_copy(out=s_gB[:],
                                                 in_=pb[4][:, 0:nrb]),
             reads=("p4",), writes=("gB",))

        stkA = ctx.enter_context(ExitStack())
        s_diagF = sb("s_diagF", [128, B], F32, stkA)
        s_iotaM = sb("s_iotaM", [1, B], F32, stkA)
        s_iotaK = sb("s_iotaK", [128, KH], F32, stkA)
        s_iotaD = sb("s_iotaD", [128, KH * B], F32, stkA)
        s_S = sb("s_S", [128, H * 128], F32, stkA)
        s_Sbf = sb("s_Sbf", [128, H * 128], BF16, stkA)
        s_xbf = sb("s_xbf", [128, TT], BF16, stkA)
        s_xsq = sb("s_xsq", [128, TT], BF16, stkA)
        s_mrep = sb("s_mrep", [128, TT], F32, stkA)
        s_rrep = sb("s_rrep", [128, TT], F32, stkA)
        s_kd = sb("s_kd", [128, 128], BF16, stkA)
        s_qd = sb("s_qd", [128, B], BF16, stkA)
        s_sT = sb("s_sT", [128, KH * B], BF16, stkA)
        s_atile = sb("s_atile", [128, B], BF16, stkA)
        s_gtile = sb("s_gtile", [128, TT], BF16, stkA)
        s_ngt = sb("s_ngt", [128, TT], F32, stkA)
        s_xn = sb("s_xn", [128, C * TOK], BF16, stkA)
        s_xblk = sb("s_xblk", [128, C * min(TT, 256)], F32, stkA)
        s_qdecR = sb("s_qdecR", [128, H * B], BF16, stkA)
        s_diag = sb("s_diag", [128, KH * H * B], BF16, stkA)

        p.op(G, lambda e: e.iota(out=s_iotaM[:], pattern=[[1, B]], base=1,
                                 channel_multiplier=0,
                                 allow_small_or_imprecise_dtypes=True),
             writes=("iotaM",))
        for half in range(KH):
            p.op(G, lambda e, hf=half: e.iota(
                    out=s_iotaK[:, hf:hf + 1], pattern=[[0, 1]],
                    base=1 + 128 * hf - B, channel_multiplier=1,
                    allow_small_or_imprecise_dtypes=True),
                 writes=(("iotaK", half),))
            p.op(G, lambda e, hf=half: e.iota(
                    out=s_iotaD[:, hf * B:(hf + 1) * B], pattern=[[1, B]],
                    base=-128 * hf, channel_multiplier=-1,
                    allow_small_or_imprecise_dtypes=True),
                 writes=(("iotaD", half),))

        qrow = s_rows[0:1, 3 * TT:3 * TT + B]
        for h in range(H):
            sh = slope_f[h]
            p.op(S_, lambda e, sh=sh: e.activation(
                    out=qrow, in_=s_iotaM[:], func=AF.Exp, scale=-sh),
                 reads=("iotaM",), writes=("qrow",))
            p.op(T, lambda e: e.matmul(
                    out=pb[4][:, 0:B], lhsT=s_ones1[:], rhs=qrow,
                    start=True, stop=True),
                 reads=("qrow", "ones1"), writes=("p4",))
            p.op(V, lambda e, h=h: e.tensor_copy(
                    out=s_qdecR[:, h * B:(h + 1) * B], in_=pb[4][:, 0:B]),
                 reads=("p4",), writes=(("qdecR", h),))
            for half in range(KH):
                p.op(S_, lambda e, h=h, sh=sh, hf=half: e.activation(
                        out=s_kdec[:, hf * H + h:hf * H + h + 1],
                        in_=s_iotaK[:, hf:hf + 1], func=AF.Exp, scale=sh),
                     reads=(("iotaK", half),), writes=(("kdec", h, half),))
                p.op(S_, lambda e, h=h, sh=sh, hf=half: e.activation(
                        out=s_diagF[:], in_=s_iotaD[:, hf * B:(hf + 1) * B],
                        func=AF.Exp, scale=-sh),
                     reads=(("iotaD", half),), writes=("diagF",))
                p.op(G, lambda e, h=h, hf=half: e.affine_select(
                        out=s_diag[:, (hf * H + h) * B:(hf * H + h + 1) * B],
                        in_=s_diagF[:], pattern=[[1, B]],
                        compare_op=ALU.is_ge, fill=0.0, base=-128 * hf,
                        channel_multiplier=-1),
                     reads=("diagF",), writes=(("diag", h, half),))

        p.op(V, lambda e: e.memset(s_S[:], 0.0),
             writes=tuple(("S", h) for h in range(H)))

        mat_steps = []

        mat_emitted = set()

        MAT_FLAG_IDX = {("wgm", 0): 0, ("wum", 0): 1, ("wgm", 1): 2,
                        ("wum", 1): 3, ("wdm", 0): 4, ("wdm", 1): 5}

        def emit_mat(matname, IDXT, WM, nch_local, rows_other, width,
                     halves=(0, 1)):
            hw = width // 2
            for hh in halves:
                for ch0 in range(0, nch_local, 2):
                    lanes = [ch for ch in (ch0, ch0 + 1) if ch < nch_local]
                    for li, ch in enumerate(lanes):
                        def load(e, ch=ch, hh=hh, hw=hw, li=li, IDXT=IDXT):
                            return e.dma_start(
                                out=s_midx[:, li * MHALF:li * MHALF + hw],
                                in_=IDXT[ch * 128:(ch + 1) * 128,
                                         hh * hw:(hh + 1) * hw])
                        mat_steps.append(("dma", G, load, (),
                                          (("midx", li),), "si"))
                    for li, ch in enumerate(lanes):
                        def ts0(e, hw=hw, li=li):
                            return e.tensor_scalar(
                                out=s_macc[:, li * MHALF:li * MHALF + hw],
                                in0=s_midx[:, li * MHALF:li * MHALF + hw],
                                scalar1=0.0, scalar2=lut_f[0],
                                op0=ALU.is_equal, op1=ALU.mult)
                        mat_steps.append(("op", V, ts0, (("midx", li),),
                                          (("macc", li),), None))
                    for t in range(1, LUTN):
                        for li, ch in enumerate(lanes):
                            tb_ = (s_mt0, s_mt1)[li]
                            def ts(e, t=t, hw=hw, li=li, tb_=tb_):
                                return e.tensor_scalar(
                                    out=tb_[:, 0:hw],
                                    in0=s_midx[:, li * MHALF:li * MHALF + hw],
                                    scalar1=float(t), scalar2=lut_f[t],
                                    op0=ALU.is_equal, op1=ALU.mult)
                            mat_steps.append(("op", V, ts, (("midx", li),),
                                              (("mt", li),), None))
                        for li, ch in enumerate(lanes):
                            tb_ = (s_mt0, s_mt1)[li]
                            def tt(e, hw=hw, li=li, tb_=tb_):
                                return e.tensor_tensor(
                                    out=s_macc[:, li * MHALF:li * MHALF + hw],
                                    in0=s_macc[:, li * MHALF:li * MHALF + hw],
                                    in1=tb_[:, 0:hw], op=ALU.add)
                            mat_steps.append(("op", V, tt,
                                              (("mt", li), ("macc", li)),
                                              (("macc", li),), None))
                    for li, ch in enumerate(lanes):
                        def store(e, ch=ch, WM=WM, hh=hh, hw=hw, li=li,
                                  ro=rows_other):
                            with e.If_eq(rme_s, 0):
                                i1 = e.dma_start(
                                    out=WM[ch * 128:(ch + 1) * 128,
                                           hh * hw:(hh + 1) * hw],
                                    in_=s_macc[:, li * MHALF:li * MHALF + hw])
                            with e.Else():
                                i2 = e.dma_start(
                                    out=WM[ro + ch * 128:ro + (ch + 1) * 128,
                                           hh * hw:(hh + 1) * hw],
                                    in_=s_macc[:, li * MHALF:li * MHALF + hw])
                            return _DualInst(i1, i2)
                        mat_steps.append(("dma", SY, store, (("macc", li),),
                                          ((matname, ch, hh),), "sm"))
                fi = MAT_FLAG_IDX[(matname, hh)]
                def flag(e, fi=fi):
                    with e.If_eq(rme_s, 0):
                        i1 = e.dma_start(out=FLG[0:1, fi * 16:(fi + 1) * 16],
                                         in_=s_mag[:])
                    with e.Else():
                        i2 = e.dma_start(
                            out=FLG[0:1, 96 + fi * 16:96 + (fi + 1) * 16],
                            in_=s_mag[:])
                    return _DualInst(i1, i2)
                mat_steps.append(("dma", SY, flag, ("mag", "rme_s"),
                                  ((matname, "done", hh),), "sm"))

        mat_pos = [0]

        def mat_step(n=1):
            for _ in range(n):
                if mat_pos[0] >= len(mat_steps):
                    return
                kind, eng, fn, rd, wr, stream = mat_steps[mat_pos[0]]
                mat_pos[0] += 1
                if kind == "dma":
                    p.dma(eng, fn, reads=rd, writes=wr, stream=stream)
                else:
                    p.op(eng, fn, reads=rd, writes=wr)
                for k in wr:
                    if isinstance(k, tuple) and len(k) == 3 and \
                            k[0] in ("wgm", "wum", "wdm"):
                        mat_emitted.add(k)

        polled = set()

        def mat_require_mh(pairs):
            for m, hh in pairs:
                while (m, "done", hh) not in mat_emitted:
                    if mat_pos[0] >= len(mat_steps):
                        raise AssertionError(f"mat exhausted for {(m, hh)}")
                    mat_step(1)
            for m, hh in pairs:
                if (m, hh) in polled:
                    continue
                polled.add((m, hh))
                fi = MAT_FLAG_IDX[(m, hh)]

                def poll(e, fi=fi):
                    sl = s_flag[0:1, fi * 16:(fi + 1) * 16]
                    with e.If_eq(rme_g, 0):
                        e.dma_start(out=sl,
                                    in_=FLG[0:1, 96 + fi * 16:
                                            96 + (fi + 1) * 16]
                                    ).then_inc(p.sems["sp"], 16)
                    with e.Else():
                        e.dma_start(out=sl,
                                    in_=FLG[0:1, fi * 16:(fi + 1) * 16]
                                    ).then_inc(p.sems["sp"], 16)
                    e.reg_mov(p_pz, 1)
                    with e.While(p_pz):
                        e.tensor_copy(out=s_dly[:], in_=s_dly[:])
                        with e.If_eq(rme_g, 0):
                            e.dma_start(out=sl,
                                        in_=FLG[0:1, 96 + fi * 16:
                                                96 + (fi + 1) * 16]
                                        ).then_inc(p.sems["sp"], 16)
                        with e.Else():
                            e.dma_start(out=sl,
                                        in_=FLG[0:1, fi * 16:(fi + 1) * 16]
                                        ).then_inc(p.sems["sp"], 16)
                        e.tensor_copy(out=s_dly[:], in_=s_dly[:])
                        e.reg_load(p_pr, s_flag[0:1, fi * 16:fi * 16 + 1])
                        e.reg_alu(p_pz, p_pr, MAGICI, ALU.not_equal)
                    return e.engine_nop()
                p._sem("sp")
                p.op(G, poll, reads=("mag", "dly", "rme_g"),
                     writes=(("poll", m, hh), "flagbuf"))

        def mat_flush():
            while mat_pos[0] < len(mat_steps):
                mat_step(1)

        out_streams = ["so0", "so1", "so2", "so3"]
        out_rr = [0]

        def out_stream():
            s = out_streams[out_rr[0] % 4]
            out_rr[0] += 1
            return s

        LTT = min(TT, 256)
        NLT = TT // LTT

        def ln_tile(tok0, dst_off, tag, old_tag, s_xn, s_xblk):
            p.dma(SY, lambda e: e.dma_start(
                    out=s_xblk[:].rearrange("p (c n) -> p c n", c=C),
                    in_=XTr[:, :, tok0:tok0 + LTT]),
                  writes=("xblk",), stream="sx")
            for ch in range(C):
                xc = s_xblk[:, ch * LTT:(ch + 1) * LTT]
                p.op(V, lambda e, xc=xc: e.tensor_copy(out=s_xbf[:, 0:LTT],
                                                       in_=xc),
                     reads=("xblk",), writes=("xbf",))
                p.op(T, lambda e, ch=ch: e.matmul(
                        out=pb[4][0:1, 0:LTT], lhsT=s_onesk[:],
                        rhs=s_xbf[:, 0:LTT],
                        start=(ch == 0), stop=(ch == C - 1)),
                     reads=("xbf", "onesk"), writes=("p4",),
                     accum_key="p4", accum_continue=(ch != 0))
                p.op(S_, lambda e: e.activation(out=s_xsq[:, 0:LTT],
                                                in_=s_xbf[:, 0:LTT],
                                                func=AF.Square),
                     reads=("xbf",), writes=("xsq",))
                p.op(T, lambda e, ch=ch: e.matmul(
                        out=pb[5][0:1, 0:LTT], lhsT=s_onesk[:],
                        rhs=s_xsq[:, 0:LTT],
                        start=(ch == 0), stop=(ch == C - 1)),
                     reads=("xsq", "onesk"), writes=("p5",),
                     accum_key="p5", accum_continue=(ch != 0))
                mat_step(1)
            r_m = s_rows[0:1, 0:LTT]
            r_v = s_rows[0:1, TT:TT + LTT]
            r_r = s_rows[0:1, 2 * TT:2 * TT + LTT]
            r_mm = s_rows[0:1, 3 * TT:3 * TT + LTT]
            p.op(V, lambda e: e.tensor_scalar(
                    out=r_m, in0=pb[4][0:1, 0:LTT], scalar1=1.0 / D,
                    scalar2=None, op0=ALU.mult, op1=ALU.bypass),
                 reads=("p4",), writes=("r_m",))
            p.op(V, lambda e: e.tensor_scalar(
                    out=r_v, in0=pb[5][0:1, 0:LTT], scalar1=1.0 / D,
                    scalar2=None, op0=ALU.mult, op1=ALU.bypass),
                 reads=("p5",), writes=("r_v",))
            p.op(V, lambda e: e.tensor_tensor(out=r_mm, in0=r_m, in1=r_m,
                                              op=ALU.mult),
                 reads=("r_m",), writes=("r_mm", "qrow"))
            p.op(V, lambda e: e.scalar_tensor_tensor(
                    out=r_v, in0=r_v, scalar=1e-5, in1=r_mm,
                    op0=ALU.add, op1=ALU.subtract),
                 reads=("r_v", "r_mm"), writes=("r_v",))
            p.op(S_, lambda e: e.activation(out=r_r, in_=r_v, func=AF.Ln),
                 reads=("r_v",), writes=("r_r",))
            p.op(S_, lambda e: e.activation(out=r_r, in_=r_r, func=AF.Exp,
                                            scale=-0.5),
                 reads=("r_r",), writes=("r_r",))
            p.op(V, lambda e: e.tensor_scalar(
                    out=r_m, in0=r_m, scalar1=-1.0, scalar2=None,
                    op0=ALU.mult, op1=ALU.bypass),
                 reads=("r_m",), writes=("r_m",))
            p.op(T, lambda e: e.matmul(out=pb[4][:, 0:LTT], lhsT=s_ones1[:],
                                       rhs=r_m, start=True, stop=True),
                 reads=("r_m", "ones1"), writes=("p4",))
            p.op(V, lambda e: e.tensor_copy(out=s_mrep[:, 0:LTT],
                                            in_=pb[4][:, 0:LTT]),
                 reads=("p4",), writes=("mrep",))
            p.op(T, lambda e: e.matmul(out=pb[5][:, 0:LTT], lhsT=s_ones1[:],
                                       rhs=r_r, start=True, stop=True),
                 reads=("r_r", "ones1"), writes=("p5",))
            p.op(V, lambda e: e.tensor_copy(out=s_rrep[:, 0:LTT],
                                            in_=pb[5][:, 0:LTT]),
                 reads=("p5",), writes=("rrep",))
            for ch in range(C):
                xc = s_xblk[:, ch * LTT:(ch + 1) * LTT]
                p.op(V, lambda e, xc=xc: e.tensor_tensor(
                        out=xc, in0=xc, in1=s_mrep[:, 0:LTT], op=ALU.add),
                     reads=("xblk", "mrep"), writes=("xblk",))
                wkeys = (("xn", tag, ch),)
                if old_tag is not None:
                    wkeys = wkeys + (("xn", old_tag, ch),)
                p.op(V, lambda e, xc=xc, ch=ch: e.tensor_tensor(
                        out=s_xn[:, ch * TOK + dst_off:
                                 ch * TOK + dst_off + LTT],
                        in0=xc, in1=s_rrep[:, 0:LTT], op=ALU.mult),
                     reads=("xblk", "rrep"), writes=wkeys)
                mat_step(1)

        wq_bufs = [s_w0, s_w1]

        def load_wslab(WSRCr, col0, width, buf_i, tag):
            buf = wq_bufs[buf_i]
            dst = buf[:, 0:C * SLAB].rearrange("p (c n) -> p c n", c=C)
            new = tuple((tag, f) for f in range(C))
            old = cur_keys.get(f"w{buf_i}", ())
            cur_keys[f"w{buf_i}"] = new
            p.dma(G, lambda e, dst=dst, col0=col0, width=width, WSRCr=WSRCr:
                  e.dma_start(out=dst[:, :, 0:width],
                              in_=WSRCr[:, :, col0:col0 + width]),
                  writes=new + tuple(old), stream=f"sw{buf_i}")

        emit_mat("wgm", GIDXT, WGM, CL, CL * 128, I, halves=(0,))
        emit_mat("wum", UIDXT, WUM, CL, CL * 128, I, halves=(0,))
        emit_mat("wgm", GIDXT, WGM, CL, CL * 128, I, halves=(1,))
        emit_mat("wum", UIDXT, WUM, CL, CL * 128, I, halves=(1,))
        emit_mat("wdm", DIDXT, WDM, ICL, ICL * 128, D)

        stkA2 = ctx.enter_context(ExitStack())
        s_q = sb("s_q", [128, HGH * TOK], BF16, stkA2)
        s_kf = sb("s_kf", [128, HGH * TOK], BF16, stkA2)
        s_ktm = sb("s_ktm", [128, TB * GW], BF16, stkA2)
        s_vtm = sb("s_vtm", [128, TB * GW], BF16, stkA2)

        RCH = RB // NB
        for rc in range(RCH):
            old_tag = None if rc == 0 else ("r", rc - 1)
            for tt in range(NTT):
                for lt in range(NLT):
                    ln_tile(rc * TOK + tt * TT + lt * LTT,
                            tt * TT + lt * LTT, ("r", rc), old_tag,
                            s_xn, s_xblk)
            for hg in range(NHG):
                for kv in range(2):
                    col0 = (1 + kv) * D + hg * GW
                    bi = kv
                    load_wslab(WQKVTr, col0, GW, bi, ("wkv", rc, hg, kv))
                    dst = s_ktm if kv == 0 else s_vtm
                    nm = "rk" if kv == 0 else "rv"
                    for tb in range(TB):
                        pbi = tb % 2
                        for f in range(C):
                            p.op(T, lambda e, f=f, tb=tb, bi=bi, pbi=pbi,
                                 rc=rc: e.matmul(
                                    out=pb[pbi][:, 0:GW],
                                    lhsT=s_xn[:, f * TOK + tb * 128:
                                              f * TOK + (tb + 1) * 128],
                                    rhs=wq_bufs[bi][:, f * SLAB:
                                                    f * SLAB + GW],
                                    start=(f == 0), stop=(f == C - 1)),
                                 reads=((("wkv", rc, hg, bi), f),
                                        ("xn", ("r", rc), f)),
                                 writes=(f"p{pbi}",), accum_key=f"p{pbi}",
                                 accum_continue=(f != 0), inc=(f == C - 1))
                        chunks = list(range(HGH))
                        wkeys = tuple((nm, rc, hg, tb, hl) for hl in chunks)
                        old = rekey_regions("ktmbuf" if kv == 0 else "vtmbuf",
                                            tb, chunks, wkeys)
                        p.op(S_, lambda e, tb=tb, pbi=pbi, dst=dst:
                             e.activation(out=dst[:, tb * GW:(tb + 1) * GW],
                                          in_=pb[pbi][:, 0:GW],
                                          func=AF.Silu),
                             reads=(f"p{pbi}",), writes=wkeys + old)
                        mat_step(2)
                for nb_ in range(NB):
                    rb = rc * NB + nb_
                    p.op(V, lambda e, rb=rb: e.tensor_scalar(
                            out=s_frow[:], in0=s_bdm1[:],
                            scalar1=s_gates[0:1, rb:rb + 1],
                            scalar2=1.0, op0=ALU.mult, op1=ALU.add),
                         reads=("bdm1", "gates"), writes=("frow",))
                    p.op(T, lambda e: e.matmul(
                            out=pb[4][:, 0:H], lhsT=s_ones1[:], rhs=s_frow[:],
                            start=True, stop=True),
                         reads=("frow", "ones1"), writes=("p4",))
                    p.op(V, lambda e: e.tensor_copy(out=s_fbc[:],
                                                    in_=pb[4][:, 0:H]),
                         reads=("p4",), writes=("fbc",))
                    p.op(V, lambda e, rb=rb: e.tensor_scalar(
                            out=s_gkdec[:], in0=s_kdec[:],
                            scalar1=s_gB[:, rb:rb + 1], scalar2=None,
                            op0=ALU.mult, op1=ALU.bypass),
                         reads=tuple(("kdec", h, hf) for h in range(H)
                                     for hf in range(KH)) + ("gB",),
                         writes=("gkdec",))
                    for hl in range(HGH):
                        h = hg * HGH + hl
                        for half in range(KH):
                            tb = nb_ * KH + half
                            p.op(V, lambda e, tb=tb, h=h, hl=hl, half=half,
                                 rc=rc, hg=hg: e.tensor_scalar(
                                    out=s_kd[:],
                                    in0=s_ktm[:, tb * GW + hl * 128:
                                              tb * GW + (hl + 1) * 128],
                                    scalar1=s_gkdec[:, half * H + h:
                                                    half * H + h + 1],
                                    scalar2=None, op0=ALU.mult,
                                    op1=ALU.bypass),
                                 reads=(("rk", rc, hg, tb, hl), "gkdec"),
                                 writes=("kd",))
                            p.op(T, lambda e, tb=tb, hl=hl, half=half,
                                 rc=rc, hg=hg: e.matmul(
                                    out=pb[2][:, 0:128], lhsT=s_kd[:],
                                    rhs=s_vtm[:, tb * GW + hl * 128:
                                              tb * GW + (hl + 1) * 128],
                                    start=(half == 0),
                                    stop=(half == KH - 1)),
                                 reads=("kd", ("rv", rc, hg, tb, hl)),
                                 writes=("p2",), accum_key="p2",
                                 accum_continue=(half != 0))
                        p.op(V, lambda e, h=h: e.scalar_tensor_tensor(
                                out=s_S[:, h * 128:(h + 1) * 128],
                                in0=s_S[:, h * 128:(h + 1) * 128],
                                scalar=s_fbc[:, h:h + 1],
                                in1=pb[2][:, 0:128], op0=ALU.mult,
                                op1=ALU.add),
                             reads=(("S", h), "fbc", "p2"),
                             writes=(("S", h),))
                        mat_step(1)

        for tt in range(NTT):
            for lt in range(NLT):
                ln_tile(3 * TOK + tt * TT + lt * LTT, tt * TT + lt * LTT,
                        "own", ("r", RCH - 1) if RCH > 0 else None,
                        s_xn, s_xblk)

        for sl in range(D // SLAB):
            bi = sl % 2
            load_wslab(WGTr, sl * SLAB, SLAB, bi, ("gatew", sl))
            for th in range(NTT):
                for oc in range(SLAB // 128):
                    o = sl * SLAB + oc * 128
                    pbi = oc % 2
                    for f in range(C):
                        p.op(T, lambda e, f=f, th=th, oc=oc, bi=bi, pbi=pbi:
                             e.matmul(
                                out=pb[pbi][:, 0:TT],
                                lhsT=wq_bufs[bi][:, f * SLAB + oc * 128:
                                                 f * SLAB + (oc + 1) * 128],
                                rhs=s_xn[:, f * TOK + th * TT:
                                         f * TOK + th * TT + TT],
                                start=(f == 0), stop=(f == C - 1)),
                             reads=((("gatew", sl), f), ("xn", "own", f)),
                             writes=(f"p{pbi}",), accum_key=f"p{pbi}",
                             accum_continue=(f != 0), inc=(f == C - 1))
                    ch = o // 128
                    p.op(S_, lambda e, pbi=pbi: e.activation(
                            out=s_gtile[:], in_=pb[pbi][:, 0:TT],
                            func=AF.Sigmoid),
                         reads=(f"p{pbi}",), writes=("gtile",))
                    p.dma(SY, lambda e, ch=ch, th=th: e.dma_start(
                            out=GATS[ch * 128:(ch + 1) * 128,
                                     th * TT:th * TT + TT],
                            in_=s_gtile[:]),
                          reads=("gtile",), writes=(("gats", ch, th),),
                          stream="sst")
                    mat_step(2)

        p.dma(SY, lambda e: e.dma_start(
                out=XNSr[:, :, :],
                in_=s_xn[:].rearrange("p (c n) -> p c n", c=C)),
              reads=tuple(("xn", "own", ch) for ch in range(C)),
              writes=("xns",), stream="sst")

        for hg in range(NHG):
            for qk, dstbuf, dsttag in ((0, s_q, "q"), (1, s_kf, "kf")):
                col0 = qk * D + hg * GW
                bi = qk
                load_wslab(WQKVTr, col0, GW, bi, (dsttag, "w", hg))
                for th in range(NTT):
                    for oc in range(GW // 128):
                        pbi = oc % 2
                        for f in range(C):
                            p.op(T, lambda e, f=f, th=th, oc=oc, bi=bi,
                                 pbi=pbi: e.matmul(
                                    out=pb[pbi][:, 0:TT],
                                    lhsT=wq_bufs[bi][:, f * SLAB + oc * 128:
                                                     f * SLAB + (oc+1)*128],
                                    rhs=s_xn[:, f * TOK + th * TT:
                                             f * TOK + th * TT + TT],
                                    start=(f == 0), stop=(f == C - 1)),
                                 reads=(((dsttag, "w", hg), f),
                                        ("xn", "own", f)),
                                 writes=(f"p{pbi}",), accum_key=f"p{pbi}",
                                 accum_continue=(f != 0), inc=(f == C - 1))
                        wk = (dsttag, hg, oc, th)
                        rid = (dsttag + "buf", oc, th)
                        old = cur_keys.get(rid, ())
                        cur_keys[rid] = (wk,)
                        p.op(S_, lambda e, oc=oc, th=th, pbi=pbi,
                             dstbuf=dstbuf: e.activation(
                                out=dstbuf[:, oc * TOK + th * TT:
                                           oc * TOK + th * TT + TT],
                                in_=pb[pbi][:, 0:TT], func=AF.Silu),
                             reads=(f"p{pbi}",),
                             writes=(wk,) + tuple(old))
                        mat_step(2)
            load_wslab(WQKVTr, 2 * D + hg * GW, GW, 0, ("vw", hg))
            for tb in range(TB):
                pbi = tb % 2
                for f in range(C):
                    p.op(T, lambda e, f=f, tb=tb, pbi=pbi: e.matmul(
                            out=pb[pbi][:, 0:GW],
                            lhsT=s_xn[:, f * TOK + tb * 128:
                                      f * TOK + (tb + 1) * 128],
                            rhs=wq_bufs[0][:, f * SLAB:f * SLAB + GW],
                            start=(f == 0), stop=(f == C - 1)),
                         reads=((("vw", hg), f), ("xn", "own", f)),
                         writes=(f"p{pbi}",), accum_key=f"p{pbi}",
                         accum_continue=(f != 0), inc=(f == C - 1))
                chunks = list(range(HGH))
                wkeys = tuple(("vtm", hg, tb, hl) for hl in chunks)
                old = rekey_regions("vtmbuf", tb, chunks, wkeys)
                p.op(S_, lambda e, tb=tb, pbi=pbi: e.activation(
                        out=s_vtm[:, tb * GW:(tb + 1) * GW],
                        in_=pb[pbi][:, 0:GW], func=AF.Silu),
                     reads=(f"p{pbi}",), writes=wkeys + old)
                mat_step(2)
            for tb in range(TB):
                for hl in range(HGH):
                    old = rekey_regions("ktmbuf", tb, [hl],
                                        [("ktmo", hg, tb, hl)])
                    p.op(T, lambda e, tb=tb, hl=hl: e.transpose(
                            out=pT[:, 0:128],
                            in_=s_kf[:, hl * TOK + tb * 128:
                                     hl * TOK + (tb + 1) * 128],
                            identity=s_id[:]),
                         reads=(("kf", hg, hl, (tb * 128) // TT), "id"),
                         writes=("pT",))
                    p.op(V, lambda e, tb=tb, hl=hl: e.tensor_copy(
                            out=s_ktm[:, tb * GW + hl * 128:
                                      tb * GW + (hl + 1) * 128],
                            in_=pT[:, 0:128]),
                         reads=("pT",), writes=(("ktmo", hg, tb, hl),) +
                         tuple(old))
                    mat_step(1)
            for hl in range(HGH):
                h = hg * HGH + hl
                for blk in range(NB):
                    t0 = blk * B
                    th_of = t0 // TT
                    p.op(V, lambda e, hl=hl, h=h, t0=t0: e.tensor_tensor(
                            out=s_qd[:],
                            in0=s_q[:, hl * TOK + t0:hl * TOK + t0 + B],
                            in1=s_qdecR[:, h * B:(h + 1) * B], op=ALU.mult),
                         reads=(("q", hg, hl, th_of), ("qdecR", h)),
                         writes=("qdt",))
                    p.op(V, lambda e, h=h: e.tensor_copy(
                            out=s_Sbf[:, h * 128:(h + 1) * 128],
                            in_=s_S[:, h * 128:(h + 1) * 128]),
                         reads=(("S", h),), writes=(("Sbf", h),))
                    for half in range(KH):
                        p.op(T, lambda e, hl=hl, t0=t0, half=half: e.matmul(
                                out=pb[half][:, 0:B],
                                lhsT=s_kf[:, hl * TOK + t0 + half * 128:
                                          hl * TOK + t0 + half * 128 + 128],
                                rhs=s_q[:, hl * TOK + t0:hl * TOK + t0 + B],
                                start=True, stop=True),
                             reads=(("kf", hg, hl, th_of),
                                    ("q", hg, hl, th_of)),
                             writes=(f"p{half}",))
                        p.op(V, lambda e, h=h, half=half: e.tensor_tensor(
                                out=s_sT[:, half * B:(half + 1) * B],
                                in0=pb[half][:, 0:B],
                                in1=s_diag[:, (half * H + h) * B:
                                           (half * H + h + 1) * B],
                                op=ALU.mult),
                             reads=(f"p{half}", ("diag", h, half)),
                             writes=(("sT", half),))
                    p.op(T, lambda e, h=h: e.matmul(
                            out=pb[2][:, 0:B],
                            lhsT=s_Sbf[:, h * 128:(h + 1) * 128],
                            rhs=s_qd[:], start=True, stop=False),
                         reads=(("Sbf", h), "qdt"), writes=("p2",),
                         accum_key="p2", inc=False)
                    for half in range(KH):
                        tb = blk * KH + half
                        p.op(T, lambda e, hl=hl, tb=tb, half=half, hg=hg:
                             e.matmul(
                                out=pb[2][:, 0:B],
                                lhsT=s_vtm[:, tb * GW + hl * 128:
                                           tb * GW + (hl + 1) * 128],
                                rhs=s_sT[:, half * B:(half + 1) * B],
                                start=False, stop=(half == KH - 1)),
                             reads=(("vtm", hg, tb, hl), ("sT", half)),
                             writes=("p2",), accum_key="p2",
                             accum_continue=True, inc=(half == KH - 1))
                    p.op(V, lambda e: e.tensor_copy(out=s_atile[:],
                                                    in_=pb[2][:, 0:B]),
                         reads=("p2",), writes=("atile",))
                    p.dma(SY, lambda e, h=h, t0=t0: e.dma_start(
                            out=ATTS[h * 128:(h + 1) * 128, t0:t0 + B],
                            in_=s_atile[:]),
                          reads=("atile",), writes=(("atts", h, blk),),
                          stream="sat")
                    for half in range(KH):
                        tb = blk * KH + half
                        p.op(V, lambda e, h=h, hl=hl, tb=tb, half=half,
                             hg=hg: e.tensor_scalar(
                                out=s_kd[:],
                                in0=s_ktm[:, tb * GW + hl * 128:
                                          tb * GW + (hl + 1) * 128],
                                scalar1=s_kdec[:, half * H + h:
                                               half * H + h + 1],
                                scalar2=None, op0=ALU.mult, op1=ALU.bypass),
                             reads=(("ktmo", hg, tb, hl), ("kdec", h, half)),
                             writes=("kd",))
                        p.op(T, lambda e, hl=hl, tb=tb, half=half, hg=hg:
                             e.matmul(
                                out=pb[3][:, 0:128], lhsT=s_kd[:],
                                rhs=s_vtm[:, tb * GW + hl * 128:
                                          tb * GW + (hl + 1) * 128],
                                start=(half == 0), stop=(half == KH - 1)),
                             reads=("kd", ("vtm", hg, tb, hl)),
                             writes=("p3",), accum_key="p3",
                             accum_continue=(half != 0))
                    p.op(V, lambda e, h=h: e.scalar_tensor_tensor(
                            out=s_S[:, h * 128:(h + 1) * 128],
                            in0=s_S[:, h * 128:(h + 1) * 128],
                            scalar=s_bdB[:, h:h + 1], in1=pb[3][:, 0:128],
                            op0=ALU.mult, op1=ALU.add),
                         reads=(("S", h), "bdB", "p3"), writes=(("S", h),))
                    mat_step(4)

        stkA2.close()
        phase_barrier()

        stkA3 = ctx.enter_context(ExitStack())
        s_ng = sb("s_ng", [128, C * TOK], BF16, stkA3)
        for th in range(NTT):
            p.dma(SY, lambda e, th=th: e.dma_start(
                    out=s_ng[:].rearrange("p (c n) -> p c n", c=C)
                    [:, :, th * TT:th * TT + TT],
                    in_=ATTSr[:, :, th * TT:th * TT + TT]),
                  reads=tuple(("atts", h, blk) for h in range(H)
                              for blk in range(NB)),
                  writes=tuple(("attn", ch, th) for ch in range(C)),
                  stream="sar")
            for ch in range(C):
                p.op(S_, lambda e, ch=ch, th=th: e.activation(
                        out=s_xsq[:, 0:TT],
                        in_=s_ng[:, ch * TOK + th * TT:
                                 ch * TOK + th * TT + TT],
                        func=AF.Square),
                     reads=(("attn", ch, th),), writes=("xsq",))
                p.op(T, lambda e, ch=ch: e.matmul(
                        out=pb[5][0:1, 0:TT], lhsT=s_onesk[:],
                        rhs=s_xsq[:, 0:TT],
                        start=(ch == 0), stop=(ch == C - 1)),
                     reads=("xsq", "onesk"), writes=("p5",),
                     accum_key="p5", accum_continue=(ch != 0))
                mat_step(1)
            r_v = s_rows[0:1, TT:2 * TT]
            r_r = s_rows[0:1, 2 * TT:3 * TT]
            p.op(V, lambda e: e.tensor_scalar(
                    out=r_v, in0=pb[5][0:1, 0:TT], scalar1=1.0 / D,
                    scalar2=1e-6, op0=ALU.mult, op1=ALU.add),
                 reads=("p5",), writes=("r_v",))
            p.op(S_, lambda e: e.activation(out=r_r, in_=r_v, func=AF.Ln),
                 reads=("r_v",), writes=("r_r",))
            p.op(S_, lambda e: e.activation(out=r_r, in_=r_r, func=AF.Exp,
                                            scale=-0.5),
                 reads=("r_r",), writes=("r_r",))
            p.op(T, lambda e: e.matmul(out=pb[5][:, 0:TT], lhsT=s_ones1[:],
                                       rhs=r_r, start=True, stop=True),
                 reads=("r_r", "ones1"), writes=("p5",))
            p.op(V, lambda e: e.tensor_copy(out=s_rrep[:, 0:TT],
                                            in_=pb[5][:, 0:TT]),
                 reads=("p5",), writes=("rrep",))
            for ch in range(C):
                sl_ = slice(ch * TOK + th * TT, ch * TOK + th * TT + TT)
                p.op(V, lambda e, sl_=sl_: e.tensor_tensor(
                        out=s_ngt[:], in0=s_ng[:, sl_], in1=s_rrep[:, 0:TT],
                        op=ALU.mult),
                     reads=(("attn", ch, th), "rrep"), writes=("ngt",))
                p.op(V, lambda e, ch=ch: e.tensor_scalar(
                        out=s_ngt[:], in0=s_ngt[:],
                        scalar1=s_normw[:, ch:ch + 1], scalar2=None,
                        op0=ALU.mult, op1=ALU.bypass),
                     reads=("ngt", "normw"), writes=("ngt",))
                p.dma(SY, lambda e, ch=ch, th=th: e.dma_start(
                        out=s_gtile[:],
                        in_=GATS[ch * 128:(ch + 1) * 128,
                                 th * TT:th * TT + TT]),
                      reads=(("gats", ch, th),), writes=("gtile",),
                      stream="sar")
                p.op(V, lambda e, sl_=sl_: e.tensor_tensor(
                        out=s_ng[:, sl_], in0=s_ngt[:], in1=s_gtile[:],
                        op=ALU.mult),
                     reads=("ngt", "gtile"),
                     writes=(("ng", ch, th), ("attn", ch, th)))
                mat_step(2)

        for sl in range(D // SLAB):
            bi = sl % 2
            load_wslab(WOTr, sl * SLAB, SLAB, bi, ("wow", sl))
            for th in range(NTT):
                for oc in range(SLAB // 128):
                    o = sl * SLAB + oc * 128
                    pbi = oc % 2
                    for f in range(C):
                        p.op(T, lambda e, f=f, th=th, oc=oc, bi=bi, pbi=pbi:
                             e.matmul(
                                out=pb[pbi][:, 0:TT],
                                lhsT=wq_bufs[bi][:, f * SLAB + oc * 128:
                                                 f * SLAB + (oc + 1) * 128],
                                rhs=s_ng[:, f * TOK + th * TT:
                                         f * TOK + th * TT + TT],
                                start=(f == 0), stop=(f == C - 1)),
                             reads=((("wow", sl), f), ("ng", f, th)),
                             writes=(f"p{pbi}",), accum_key=f"p{pbi}",
                             accum_continue=(f != 0), inc=(f == C - 1))
                    ch = o // 128
                    p.dma(SY, lambda e, ch=ch, th=th: e.dma_start(
                            out=s_acc[:],
                            in_=XTr[:, ch, 3 * TOK + th * TT:
                                    3 * TOK + th * TT + TT]),
                          writes=("acc",), stream="sx")
                    p.op(V, lambda e, pbi=pbi: e.tensor_tensor(
                            out=s_acc[:], in0=s_acc[:], in1=pb[pbi][:, 0:TT],
                            op=ALU.add),
                         reads=("acc", f"p{pbi}"), writes=("acc",))
                    p.dma(SY, lambda e, ch=ch, th=th: e.dma_start(
                            out=ACCS[ch * 128:(ch + 1) * 128,
                                     th * TT:th * TT + TT],
                            in_=s_acc[:]),
                          reads=("acc",), writes=(("accs", ch, th),),
                          stream="sst")
                    mat_step(2)

        stkA3.close()
        stkA.close()
        phase_barrier()

        stkB = ctx.enter_context(ExitStack())
        s_xn2 = sb("s_xn2", [128, C * TOK], BF16, stkB)
        s_hf = sb("s_hf", [128, IC * TT], BF16, stkB)

        p.dma(SY, lambda e: e.dma_start(
                out=s_xn2[:].rearrange("p (c n) -> p c n", c=C),
                in_=XNSr[:, :, :]),
              reads=("xns",),
              writes=tuple(("xn2", ch) for ch in range(C)),
              stream="sx")


        def load_islab(WSRCr, col0, width, buf, tag, stream, matkey, bufnm,
                       extra_old=()):
            dst = buf[:, 0:C * SLAB].rearrange("p (c n) -> p c n", c=C)
            new = tuple((tag, f) for f in range(C))
            old = cur_keys.get(bufnm, ())
            cur_keys[bufnm] = new
            rd_h = set()
            if col0 < I // 2:
                rd_h.add(0)
            if col0 + width > I // 2:
                rd_h.add(1)
            rds = tuple((matkey, "done", hh) for hh in rd_h) + \
                tuple(("poll", matkey, hh) for hh in rd_h)
            p.dma(G, lambda e, dst=dst, col0=col0, width=width, WSRCr=WSRCr:
                  e.dma_start(out=dst[:, :, 0:width],
                              in_=WSRCr[:, :, col0:col0 + width]),
                  reads=rds, writes=new + tuple(old) + tuple(extra_old),
                  stream=stream)

        gslabs = []
        icol = 0
        while icol < I:
            w = min(SLAB, I - icol)
            gslabs.append((icol, w))
            icol += w
        alias_pending = ()
        for th in range(NTT):
            stkU = ctx.enter_context(ExitStack())
            s_u0 = sb(f"s_u0_{th}", [128, C * SLAB], BF16, stkU)
            s_u1 = sb(f"s_u1_{th}", [128, C * SLAB], BF16, stkU)
            uslab = [s_u0, s_u1]
            cur_keys[f"u0"] = cur_keys.get("u0", ())
            for si, (col0, w) in enumerate(gslabs):
                bi = si % 2
                halves = set()
                if col0 < I // 2:
                    halves.add(0)
                if col0 + w > I // 2:
                    halves.add(1)
                mat_require_mh([(m, hh) for m in ("wgm", "wum")
                                for hh in halves])
                load_islab(WGMr, col0, w, wq_bufs[bi], ("gwf", th, si),
                           f"sw{bi}", "wgm", f"w{bi}")
                load_islab(WUMr, col0, w, uslab[bi], ("uwf", th, si),
                           f"su{bi}", "wum", f"u{bi}_{th}",
                           extra_old=(alias_pending if si < 2 else ()))
                for oc in range(w // 128):
                    ic = (col0 + oc * 128) // 128
                    for f in range(C):
                        p.op(T, lambda e, f=f, th=th, oc=oc, bi=bi, si=si:
                             e.matmul(
                                out=pb[0][:, 0:TT],
                                lhsT=wq_bufs[bi][:, f * SLAB + oc * 128:
                                                 f * SLAB + (oc + 1) * 128],
                                rhs=s_xn2[:, f * TOK + th * TT:
                                          f * TOK + th * TT + TT],
                                start=(f == 0), stop=(f == C - 1)),
                             reads=((("gwf", th, si), f), ("xn2", f)),
                             writes=("p0",), accum_key="p0",
                             accum_continue=(f != 0), inc=(f == C - 1))
                    p.op(S_, lambda e: e.activation(
                            out=s_sg[:], in_=pb[0][:, 0:TT], func=AF.Silu),
                         reads=("p0",), writes=("sg",))
                    for f in range(C):
                        p.op(T, lambda e, f=f, th=th, oc=oc, bi=bi, si=si:
                             e.matmul(
                                out=pb[1][:, 0:TT],
                                lhsT=uslab[bi][:, f * SLAB + oc * 128:
                                               f * SLAB + (oc + 1) * 128],
                                rhs=s_xn2[:, f * TOK + th * TT:
                                          f * TOK + th * TT + TT],
                                start=(f == 0), stop=(f == C - 1)),
                             reads=((("uwf", th, si), f), ("xn2", f)),
                             writes=("p1",), accum_key="p1",
                             accum_continue=(f != 0), inc=(f == C - 1))
                    p.op(V, lambda e, ic=ic: e.tensor_tensor(
                            out=s_hf[:, ic * TT:(ic + 1) * TT],
                            in0=pb[1][:, 0:TT], in1=s_sg[:], op=ALU.mult),
                         reads=("p1", "sg"), writes=(("hf", ic),))
                    mat_step(8)
            alias_pending = tuple(cur_keys.get(f"u0_{th}", ())) + \
                tuple(cur_keys.get(f"u1_{th}", ()))
            stkU.close()
            stkD = ctx.enter_context(ExitStack())
            s_dw0 = sb(f"s_dw0_{th}", [128, IC * 128], BF16, stkD)
            s_dw1 = sb(f"s_dw1_{th}", [128, IC * 128], BF16, stkD)
            dwb = [s_dw0, s_dw1]
            for oc in range(C):
                bi = oc % 2
                dhalves = {0} if (oc + 1) * 128 <= D // 2 else (
                    {1} if oc * 128 >= D // 2 else {0, 1})
                mat_require_mh([("wdm", hh) for hh in dhalves])
                dst = dwb[bi][:, 0:IC * 128].rearrange("p (c n) -> p c n",
                                                       c=IC)
                new = tuple(("dw", th, oc, f) for f in range(IC))
                old = cur_keys.get(f"dw{bi}_{th}", ())
                cur_keys[f"dw{bi}_{th}"] = new
                p.dma(G, lambda e, dst=dst, oc=oc: e.dma_start(
                        out=dst[:, :, :],
                        in_=WDMr[:, 0:IC, oc * 128:(oc + 1) * 128]),
                      reads=tuple(("wdm", "done", hh) for hh in dhalves) +
                      tuple(("poll", "wdm", hh) for hh in dhalves),
                      writes=new + tuple(old) +
                      (alias_pending if oc < 2 else ()),
                      stream=f"sd{bi}")
                for f in range(IC):
                    p.op(T, lambda e, f=f, th=th, oc=oc, bi=bi: e.matmul(
                            out=pb[2][:, 0:TT],
                            lhsT=dwb[bi][:, f * 128:(f + 1) * 128],
                            rhs=s_hf[:, f * TT:(f + 1) * TT],
                            start=(f == 0), stop=(f == IC - 1)),
                         reads=(("dw", th, oc, f), ("hf", f)),
                         writes=("p2",), accum_key="p2",
                         accum_continue=(f != 0), inc=(f == IC - 1))
                p.dma(SY, lambda e, oc=oc, th=th: e.dma_start(
                        out=s_acc[:],
                        in_=ACCS[oc * 128:(oc + 1) * 128,
                                 th * TT:th * TT + TT]),
                      reads=(("accs", oc, th),), writes=("acc",),
                      stream="sx")
                p.op(V, lambda e: e.tensor_tensor(
                        out=s_acc[:], in0=s_acc[:], in1=pb[2][:, 0:TT],
                        op=ALU.add),
                     reads=("acc", "p2"), writes=("acc",))
                mat_step(10)
                p.dma(SY, lambda e, oc=oc, th=th: e.dma_start(
                        out=OUTT[oc * 128:(oc + 1) * 128,
                                 th * TT:th * TT + TT],
                        in_=s_acc[:]),
                      reads=("acc",), writes=(), stream=out_stream())
            alias_pending = tuple(cur_keys.get(f"dw0_{th}", ())) + \
                tuple(cur_keys.get(f"dw1_{th}", ()))
            stkD.close()

        p.wait_all(SY, out_streams)
        p.finalize()
        nc._prog_counts = dict(p.count)

    return nc


def _run(cfg, x, slope, w_qkv, w_gate, w_out, norm_w, lut, gidx, uidx, didx,
         runner=run_bass_kernel_spmd):
    x = np.asarray(x)
    slope = np.asarray(slope)
    lut = np.asarray(lut)
    L0_attn_w_qkv, L0_attn_w_gate, L0_attn_w_out = w_qkv, w_gate, w_out
    L0_attn_norm_w = norm_w
    L0_ffn_gate_idx, L0_ffn_up_idx, L0_ffn_down_idx = gidx, uidx, didx

    nc = build(cfg, [float(v) for v in lut], [float(v) for v in slope],
               race_check=bool(os.environ.get("KMK_RACE")))

    wqkvT = np.ascontiguousarray(np.asarray(L0_attn_w_qkv).T)
    wgT = np.ascontiguousarray(np.asarray(L0_attn_w_gate).T)
    woT = np.ascontiguousarray(np.asarray(L0_attn_w_out).T)
    normwT = np.ascontiguousarray(
        np.asarray(L0_attn_norm_w).reshape(cfg.C, 128).T)
    gidxT = np.ascontiguousarray(np.asarray(L0_ffn_gate_idx).T)
    uidxT = np.ascontiguousarray(np.asarray(L0_ffn_up_idx).T)
    didxT = np.ascontiguousarray(np.asarray(L0_ffn_down_idx).T)
    CL = cfg.C // 2
    ICL = (cfg.IC + 1) // 2
    didxT_pad = np.zeros((2 * ICL * 128, cfg.D), dtype=np.int32)
    didxT_pad[:cfg.I, :] = didxT
    gidx_half = [np.ascontiguousarray(gidxT[m * CL * 128:(m + 1) * CL * 128])
                 for m in range(2)]
    uidx_half = [np.ascontiguousarray(uidxT[m * CL * 128:(m + 1) * CL * 128])
                 for m in range(2)]
    didx_half = [np.ascontiguousarray(
        didxT_pad[m * ICL * 128:(m + 1) * ICL * 128]) for m in range(2)]
    slope_row = np.ascontiguousarray(slope.reshape(1, cfg.H))
    xT = [np.ascontiguousarray(x[s].T) for s in range(2)]

    in_maps = []
    for core in range(NCORES):
        seq, q = core % 2, core // 2
        gates = np.zeros((1, max(cfg.RB, 2)), dtype=np.float32)
        gates[0, :q * cfg.NB] = 1.0
        xt = np.empty((cfg.D, cfg.NSEQ), dtype=np.float32)
        xt[:, 3 * cfg.TOK:] = xT[seq][:, q * cfg.TOK:(q + 1) * cfg.TOK]
        xt[:, :3 * cfg.TOK] = xT[seq][:, :3 * cfg.TOK]
        me = core % 2
        in_maps.append({
            "XT": xt, "WQKVT": wqkvT, "WGT": wgT, "WOT": woT,
            "NORMWT": normwT, "SLOPE": slope_row,
            "GIDXT": gidx_half[me], "UIDXT": uidx_half[me],
            "DIDXT": didx_half[me], "GATES": gates,
            "MEIN": np.array([[me, 1 - me]], dtype=np.int32),
        })

    res = runner(nc, in_maps, core_ids=list(range(NCORES)))

    out = np.empty((2, 4 * cfg.TOK, cfg.D), dtype=np.float32)
    for core in range(NCORES):
        seq, q = core % 2, core // 2
        outT = np.asarray(res.results[core]["OUTT"])
        out[seq, q * cfg.TOK:(q + 1) * cfg.TOK, :] = outT.T
    return out


def kernel(L0_attn_hidden_states, L0_attn_slope_rate, L0_attn_w_qkv,
           L0_attn_w_gate, L0_attn_w_out, L0_attn_norm_w, L0_ffn_lut,
           L0_ffn_gate_idx, L0_ffn_up_idx, L0_ffn_down_idx):
    return _run(CFG_FULL, L0_attn_hidden_states, L0_attn_slope_rate,
                L0_attn_w_qkv, L0_attn_w_gate, L0_attn_w_out, L0_attn_norm_w,
                L0_ffn_lut, L0_ffn_gate_idx, L0_ffn_up_idx, L0_ffn_down_idx)


# version 13 (final)
# speedup vs baseline: 1.3994x; 1.3994x over previous
import os
from contextlib import ExitStack

import numpy as np
import ml_dtypes

import concourse.bass as bass
import concourse.mybir as mybir
from concourse.bass_utils import run_bass_kernel_spmd

F32 = mybir.dt.float32
BF16 = mybir.dt.bfloat16
I32 = mybir.dt.int32
AF = mybir.ActivationFunctionType
ALU = mybir.AluOpType

NCORES = 8




class Cfg:
    def __init__(self, D, H, B, NB, I, LUTN=16):
        self.D, self.H, self.B, self.NB, self.I, self.LUTN = D, H, B, NB, I, LUTN
        self.C = D // 128
        self.IC = I // 128
        self.KH = B // 128
        self.TOK = NB * B
        self.NSEQ = 4 * self.TOK
        self.RB = 3 * NB
        self.TT = min(512, self.TOK)
        assert D == 128 * H and B % 128 == 0 and I % 128 == 0


CFG_FULL = Cfg(D=2048, H=16, B=256, NB=4, I=5504)
CFG_TINY = Cfg(D=256, H=2, B=128, NB=2, I=256)
CFG_MID = Cfg(D=2048, H=16, B=256, NB=1, I=1408)
CFG_MID2 = Cfg(D=2048, H=16, B=128, NB=1, I=256)
CFG_TINY2 = Cfg(D=256, H=2, B=256, NB=1, I=256)


class _DualInst:
    def __init__(self, a, b):
        self._insts = (a, b)

    def then_inc(self, sem, val=1, **kw):
        for i in self._insts:
            i.then_inc(sem, val, **kw)
        return self


class Prog:

    def __init__(self, nc, ctx):
        self.nc = nc
        self._ctx = ctx
        self.sems = {}
        self.count = {}
        self.hwm = {}
        self.last_write = {}
        self.readers = {}
        self.eng = {}
        self._deferred = []
        for name in ("tensor", "vector", "scalar", "gpsimd", "sync"):
            self.sems[name] = ctx.enter_context(nc.semaphore(f"sem_{name}"))
            self.count[name] = 0
            self.eng[name] = getattr(nc, name)

    def _sem(self, name):
        if name not in self.sems:
            self.sems[name] = self._ctx.enter_context(
                self.nc.semaphore(f"sem_{name}"))
            self.count[name] = 0
        return self.sems[name]

    def _wait(self, waiter, eng, sem_name, value):
        if value <= 0:
            return
        key = (waiter, sem_name)
        if self.hwm.get(key, 0) >= value:
            return
        eng.wait_ge(self.sems[sem_name], value)
        self.hwm[key] = value

    def op(self, engine_name, emit_fn, reads=(), writes=(), inc=True,
           sem_name=None, step=1, accum_key=None, accum_continue=False):
        eng = self.eng[engine_name]
        sem_name = sem_name or engine_name
        self._sem(sem_name)
        need = {}

        def add(s, v):
            if need.get(s, 0) < v:
                need[s] = v

        for k in reads:
            lw = self.last_write.get(k)
            if lw:
                add(*lw)
        for k in writes:
            lw = self.last_write.get(k)
            if lw and not (accum_continue and k == accum_key):
                add(*lw)
            for r in self.readers.get(k, ()):
                add(*r)
        for s, v in sorted(need.items()):
            self._wait(engine_name, eng, s, v)
        inst = emit_fn(eng)
        if inc:
            self.count[sem_name] += step
            inst.then_inc(self.sems[sem_name], step)
            mark = (sem_name, self.count[sem_name])
        else:
            mark = (sem_name, self.count[sem_name] + step)
            self._deferred.append(mark)
        for k in reads:
            self.readers.setdefault(k, []).append(mark)
        for k in writes:
            self.last_write[k] = mark
            self.readers[k] = []
        return mark

    def dma(self, queue, emit_fn, reads=(), writes=(), stream="dma"):
        self._sem(stream)
        eng = self.eng[queue]
        self._wait(queue, eng, stream, self.count[stream])
        return self.op(queue, emit_fn, reads=reads, writes=writes, inc=True,
                       sem_name=stream, step=16)

    def wait_all(self, engine_name, sem_names):
        eng = self.eng[engine_name]
        for s in sem_names:
            if s in self.count:
                self._wait(engine_name, eng, s, self.count[s])

    def finalize(self):
        for sem_name, v in self._deferred:
            assert v <= self.count[sem_name], (
                f"inc=False op on {sem_name} not covered "
                f"({v} > {self.count[sem_name]})")


def build(cfg, lut_vals, slope_vals, race_check=False):
    c = cfg
    D, H, B, NB, I = c.D, c.H, c.B, c.NB, c.I
    C, IC, KH, TOK, TT, RB = c.C, c.IC, c.KH, c.TOK, c.TT, c.RB
    NTT = TOK // TT
    TB = TOK // 128
    HGH = min(4, H)
    NHG = H // HGH
    GW = HGH * 128
    SLAB = min(512, D)
    LUTN = c.LUTN
    MHALF = max(I, D) // 2
    lut_f = [float(v) for v in lut_vals]
    slope_f = [float(v) for v in slope_vals]

    nc = bass.Bass("TRN2", target_bir_lowering=False, debug=False,
                   num_devices=NCORES, detect_race_conditions=race_check)

    def din(name, shape, dt):
        return nc.dram_tensor(name, shape, dt, kind="ExternalInput").ap()

    XT = din("XT", [D, c.NSEQ], F32)
    WQKVT = din("WQKVT", [D, 3 * D], F32)
    WGT = din("WGT", [D, D], F32)
    WOT = din("WOT", [D, D], F32)
    NORMWT = din("NORMWT", [128, C], F32)
    SLOPE = din("SLOPE", [1, H], F32)
    CL = C // 2
    ICL = (IC + 1) // 2
    GIDXT = din("GIDXT", [CL * 128, I], I32)
    UIDXT = din("UIDXT", [CL * 128, I], I32)
    DIDXT = din("DIDXT", [ICL * 128, D], I32)
    GATES = din("GATES", [1, max(RB, 2)], F32)
    MEIN = din("MEIN", [1, 2], I32)

    OUTT = nc.dram_tensor("OUTT", [D, TOK], F32, kind="ExternalOutput").ap()

    WGM = nc.dram_tensor("WGM", [D, I], BF16, kind="Internal",
                         addr_space="Shared").ap()
    WUM = nc.dram_tensor("WUM", [D, I], BF16, kind="Internal",
                         addr_space="Shared").ap()
    WDM = nc.dram_tensor("WDM", [2 * ICL * 128, D], BF16, kind="Internal",
                         addr_space="Shared").ap()
    FLG = nc.dram_tensor("FLG", [1, 2 * 6 * 16], I32, kind="Internal",
                         addr_space="Shared").ap()
    XNS = nc.dram_tensor("XNS", [D, TOK], BF16, kind="Internal").ap()
    GATS = nc.dram_tensor("GATS", [D, TOK], BF16, kind="Internal").ap()
    ATTS = nc.dram_tensor("ATTS", [D, TOK], BF16, kind="Internal").ap()
    ACCS = nc.dram_tensor("ACCS", [D, TOK], F32, kind="Internal").ap()

    XTr = XT.rearrange("(c p) n -> p c n", p=128)
    WQKVTr = WQKVT.rearrange("(c p) n -> p c n", p=128)
    WGTr = WGT.rearrange("(c p) n -> p c n", p=128)
    WOTr = WOT.rearrange("(c p) n -> p c n", p=128)
    WGMr = WGM.rearrange("(c p) n -> p c n", p=128)
    WUMr = WUM.rearrange("(c p) n -> p c n", p=128)
    WDMr = WDM.rearrange("(c p) n -> p c n", p=128)
    XNSr = XNS.rearrange("(c p) n -> p c n", p=128)
    ATTSr = ATTS.rearrange("(c p) n -> p c n", p=128)

    with ExitStack() as ctx:
        p = Prog(nc, ctx)

        def sb(name, shape, dt, stk=None):
            return (stk or ctx).enter_context(nc.sbuf_tensor(name, shape, dt))

        def psum(name, shape, dt):
            return ctx.enter_context(nc.psum_tensor(name, shape, dt))

        id128 = nc.inline_tensor(np.eye(128, dtype=ml_dtypes.bfloat16),
                                 "id128").ap()
        ONESK = nc.inline_tensor(np.ones((128, 1), dtype=ml_dtypes.bfloat16),
                                 "onesk").ap()
        ONES1 = nc.inline_tensor(np.ones((1, 128), dtype=np.float32),
                                 "ones1").ap()

        s_id = sb("s_id", [128, 128], BF16)
        s_onesk = sb("s_onesk", [128, 1], BF16)
        s_ones1 = sb("s_ones1", [1, 128], F32)
        s_slope = sb("s_slope", [1, H], F32)
        s_gates = sb("s_gates", [1, max(RB, 2)], F32)
        s_normw = sb("s_normw", [128, C], F32)
        s_rows = sb("s_rows", [1, 3 * TT + max(B, min(TT, 256))], F32)
        s_bdr = sb("s_bdr", [1, H], F32)
        s_bdm1 = sb("s_bdm1", [1, H], F32)
        s_frow = sb("s_frow", [1, H], F32)
        s_bdB = sb("s_bdB", [128, H], F32)
        s_gB = sb("s_gB", [128, max(RB, 2)], F32)
        s_fbc = sb("s_fbc", [128, H], F32)
        s_kdec = sb("s_kdec", [128, KH * H], F32)
        s_gkdec = sb("s_gkdec", [128, KH * H], F32)

        s_w0 = sb("s_w0", [128, C * SLAB], BF16)
        s_w1 = sb("s_w1", [128, C * SLAB], BF16)
        s_acc = sb("s_acc", [128, TT], F32)
        s_sg = sb("s_sg", [128, TT], BF16)
        s_flag = sb("s_flag", [1, 96], I32)
        s_mag = sb("s_mag", [1, 16], I32)
        s_dly = sb("s_dly", [1, 48], I32)
        s_me = sb("s_me", [1, 2], I32)
        s_midx = sb("s_midx", [128, 2 * MHALF], BF16)
        s_macc = sb("s_macc", [128, 2 * MHALF], BF16)
        s_mt0 = sb("s_mt0", [128, MHALF], BF16)
        s_mt1 = sb("s_mt1", [128, MHALF], BF16)

        pb = [psum(f"p{i}", [128, 512], F32) for i in range(6)]
        pT = psum("pT", [128, 256], BF16)

        block = ctx.enter_context(nc.Block())
        V, S_, G, T, SY = "vector", "scalar", "gpsimd", "tensor", "sync"

        cur_keys = {}

        def rekey_regions(bufnm, tb, chunks, new_keys):
            old = []
            for cidx, nk in zip(chunks, new_keys):
                rid = (bufnm, tb, cidx)
                old.extend(cur_keys.get(rid, ()))
                cur_keys[rid] = (nk,)
            return tuple(old)

        def phase_barrier():
            for e in ("tensor", "vector", "scalar", "gpsimd", "sync"):
                p.wait_all(e, list(p.count.keys()))

        for nm, (dram, sbuf) in {
            "id": (id128, s_id), "onesk": (ONESK, s_onesk),
            "ones1": (ONES1, s_ones1), "slope": (SLOPE, s_slope),
            "gates": (GATES, s_gates), "normw": (NORMWT, s_normw),
        }.items():
            p.dma(SY, lambda e, d=dram, s=sbuf: e.dma_start(out=s[:], in_=d),
                  writes=(nm,), stream="sc")

        MAGICI = 0x5A17C0DE
        MAGT = nc.inline_tensor(np.full((1, 16), MAGICI, dtype=np.int32),
                                "magt").ap()
        p.dma(SY, lambda e: e.dma_start(out=s_mag[:], in_=MAGT),
              writes=("mag",), stream="sc")
        p.dma(SY, lambda e: e.dma_start(out=s_me[:], in_=MEIN),
              writes=("mein",), stream="sc")
        p.op(G, lambda e: e.memset(s_dly[:], 1), writes=("dly",))
        p.op(G, lambda e: e.memset(s_flag[:], 0), writes=("flagbuf",))
        rme_g = ctx.enter_context(nc.gpsimd.register("rme_g"))
        rme_s = ctx.enter_context(nc.sync.register("rme_s"))
        p_pr = ctx.enter_context(nc.gpsimd.register("p_pr"))
        p_pz = ctx.enter_context(nc.gpsimd.register("p_pz"))
        p.op(G, lambda e: e.reg_load(rme_g, s_me[0:1, 0:1]),
             reads=("mein",), writes=("rme_g",))
        p.op(SY, lambda e: e.reg_load(rme_s, s_me[0:1, 0:1]),
             reads=("mein",), writes=("rme_s",))

        def zero_flags(e):
            with e.If_eq(rme_s, 0):
                i1 = e.dma_start(out=FLG[0:1, 0:96], in_=s_flag[:])
            with e.Else():
                i2 = e.dma_start(out=FLG[0:1, 96:192], in_=s_flag[:])
            return _DualInst(i1, i2)
        p.dma(SY, zero_flags, reads=("flagbuf", "rme_s"), writes=("flgz",),
              stream="sm")

        p.op(S_, lambda e: e.activation(out=s_bdr[:], in_=s_slope[:],
                                        func=AF.Exp, scale=-float(B)),
             reads=("slope",), writes=("bdr",))
        p.op(V, lambda e: e.tensor_scalar(out=s_bdm1[:], in0=s_bdr[:],
                                          scalar1=-1.0, scalar2=None,
                                          op0=ALU.add, op1=ALU.bypass),
             reads=("bdr",), writes=("bdm1",))
        p.op(T, lambda e: e.matmul(out=pb[4][:, 0:H], lhsT=s_ones1[:],
                                   rhs=s_bdr[:], start=True, stop=True),
             reads=("ones1", "bdr"), writes=("p4",))
        p.op(V, lambda e: e.tensor_copy(out=s_bdB[:], in_=pb[4][:, 0:H]),
             reads=("p4",), writes=("bdB",))
        nrb = s_gates.shape[1]
        p.op(T, lambda e, nrb=nrb: e.matmul(out=pb[4][:, 0:nrb],
                                            lhsT=s_ones1[:], rhs=s_gates[:],
                                            start=True, stop=True),
             reads=("ones1", "gates"), writes=("p4",))
        p.op(V, lambda e, nrb=nrb: e.tensor_copy(out=s_gB[:],
                                                 in_=pb[4][:, 0:nrb]),
             reads=("p4",), writes=("gB",))

        stkA = ctx.enter_context(ExitStack())
        s_diagF = sb("s_diagF", [128, B], F32, stkA)
        s_iotaM = sb("s_iotaM", [1, B], F32, stkA)
        s_iotaK = sb("s_iotaK", [128, KH], F32, stkA)
        s_iotaD = sb("s_iotaD", [128, KH * B], F32, stkA)
        s_S = sb("s_S", [128, H * 128], F32, stkA)
        s_Sbf = sb("s_Sbf", [128, H * 128], BF16, stkA)
        s_xbf = sb("s_xbf", [128, TT], BF16, stkA)
        s_xsq = sb("s_xsq", [128, TT], BF16, stkA)
        s_mrep = sb("s_mrep", [128, TT], F32, stkA)
        s_rrep = sb("s_rrep", [128, TT], F32, stkA)
        s_kd = sb("s_kd", [128, 128], BF16, stkA)
        s_qd = sb("s_qd", [128, B], BF16, stkA)
        s_sT = sb("s_sT", [128, KH * B], BF16, stkA)
        s_atile = sb("s_atile", [128, B], BF16, stkA)
        s_gtile = sb("s_gtile", [128, TT], BF16, stkA)
        s_ngt = sb("s_ngt", [128, TT], F32, stkA)
        s_xn = sb("s_xn", [128, C * TOK], BF16, stkA)
        s_xblk = sb("s_xblk", [128, C * min(TT, 256)], F32, stkA)
        s_qdecR = sb("s_qdecR", [128, H * B], BF16, stkA)
        s_diag = sb("s_diag", [128, KH * H * B], BF16, stkA)

        p.op(G, lambda e: e.iota(out=s_iotaM[:], pattern=[[1, B]], base=1,
                                 channel_multiplier=0,
                                 allow_small_or_imprecise_dtypes=True),
             writes=("iotaM",))
        for half in range(KH):
            p.op(G, lambda e, hf=half: e.iota(
                    out=s_iotaK[:, hf:hf + 1], pattern=[[0, 1]],
                    base=1 + 128 * hf - B, channel_multiplier=1,
                    allow_small_or_imprecise_dtypes=True),
                 writes=(("iotaK", half),))
            p.op(G, lambda e, hf=half: e.iota(
                    out=s_iotaD[:, hf * B:(hf + 1) * B], pattern=[[1, B]],
                    base=-128 * hf, channel_multiplier=-1,
                    allow_small_or_imprecise_dtypes=True),
                 writes=(("iotaD", half),))

        qrow = s_rows[0:1, 3 * TT:3 * TT + B]
        for h in range(H):
            sh = slope_f[h]
            p.op(S_, lambda e, sh=sh: e.activation(
                    out=qrow, in_=s_iotaM[:], func=AF.Exp, scale=-sh),
                 reads=("iotaM",), writes=("qrow",))
            p.op(T, lambda e: e.matmul(
                    out=pb[4][:, 0:B], lhsT=s_ones1[:], rhs=qrow,
                    start=True, stop=True),
                 reads=("qrow", "ones1"), writes=("p4",))
            p.op(V, lambda e, h=h: e.tensor_copy(
                    out=s_qdecR[:, h * B:(h + 1) * B], in_=pb[4][:, 0:B]),
                 reads=("p4",), writes=(("qdecR", h),))
            for half in range(KH):
                p.op(S_, lambda e, h=h, sh=sh, hf=half: e.activation(
                        out=s_kdec[:, hf * H + h:hf * H + h + 1],
                        in_=s_iotaK[:, hf:hf + 1], func=AF.Exp, scale=sh),
                     reads=(("iotaK", half),), writes=(("kdec", h, half),))
                p.op(S_, lambda e, h=h, sh=sh, hf=half: e.activation(
                        out=s_diagF[:], in_=s_iotaD[:, hf * B:(hf + 1) * B],
                        func=AF.Exp, scale=-sh),
                     reads=(("iotaD", half),), writes=("diagF",))
                p.op(G, lambda e, h=h, hf=half: e.affine_select(
                        out=s_diag[:, (hf * H + h) * B:(hf * H + h + 1) * B],
                        in_=s_diagF[:], pattern=[[1, B]],
                        compare_op=ALU.is_ge, fill=0.0, base=-128 * hf,
                        channel_multiplier=-1),
                     reads=("diagF",), writes=(("diag", h, half),))

        p.op(V, lambda e: e.memset(s_S[:], 0.0),
             writes=tuple(("S", h) for h in range(H)))

        mat_steps = []

        mat_emitted = set()

        MAT_FLAG_IDX = {("wgm", 0): 0, ("wum", 0): 1, ("wgm", 1): 2,
                        ("wum", 1): 3, ("wdm", 0): 4, ("wdm", 1): 5}

        def emit_mat(matname, IDXT, WM, nch_local, rows_other, width,
                     halves=(0, 1)):
            hw = width // 2
            for hh in halves:
                for ch0 in range(0, nch_local, 2):
                    lanes = [ch for ch in (ch0, ch0 + 1) if ch < nch_local]
                    for li, ch in enumerate(lanes):
                        def load(e, ch=ch, hh=hh, hw=hw, li=li, IDXT=IDXT):
                            return e.dma_start(
                                out=s_midx[:, li * MHALF:li * MHALF + hw],
                                in_=IDXT[ch * 128:(ch + 1) * 128,
                                         hh * hw:(hh + 1) * hw])
                        mat_steps.append(("dma", G, load, (),
                                          (("midx", li),), "si"))
                    for li, ch in enumerate(lanes):
                        def ts0(e, hw=hw, li=li):
                            return e.tensor_scalar(
                                out=s_macc[:, li * MHALF:li * MHALF + hw],
                                in0=s_midx[:, li * MHALF:li * MHALF + hw],
                                scalar1=0.0, scalar2=lut_f[0],
                                op0=ALU.is_equal, op1=ALU.mult)
                        mat_steps.append(("op", V, ts0, (("midx", li),),
                                          (("macc", li),), None))
                    for t in range(1, LUTN):
                        for li, ch in enumerate(lanes):
                            tb_ = (s_mt0, s_mt1)[li]
                            def ts(e, t=t, hw=hw, li=li, tb_=tb_):
                                return e.tensor_scalar(
                                    out=tb_[:, 0:hw],
                                    in0=s_midx[:, li * MHALF:li * MHALF + hw],
                                    scalar1=float(t), scalar2=lut_f[t],
                                    op0=ALU.is_equal, op1=ALU.mult)
                            mat_steps.append(("op", V, ts, (("midx", li),),
                                              (("mt", li),), None))
                        for li, ch in enumerate(lanes):
                            tb_ = (s_mt0, s_mt1)[li]
                            def tt(e, hw=hw, li=li, tb_=tb_):
                                return e.tensor_tensor(
                                    out=s_macc[:, li * MHALF:li * MHALF + hw],
                                    in0=s_macc[:, li * MHALF:li * MHALF + hw],
                                    in1=tb_[:, 0:hw], op=ALU.add)
                            mat_steps.append(("op", V, tt,
                                              (("mt", li), ("macc", li)),
                                              (("macc", li),), None))
                    for li, ch in enumerate(lanes):
                        def store(e, ch=ch, WM=WM, hh=hh, hw=hw, li=li,
                                  ro=rows_other):
                            with e.If_eq(rme_s, 0):
                                i1 = e.dma_start(
                                    out=WM[ch * 128:(ch + 1) * 128,
                                           hh * hw:(hh + 1) * hw],
                                    in_=s_macc[:, li * MHALF:li * MHALF + hw])
                            with e.Else():
                                i2 = e.dma_start(
                                    out=WM[ro + ch * 128:ro + (ch + 1) * 128,
                                           hh * hw:(hh + 1) * hw],
                                    in_=s_macc[:, li * MHALF:li * MHALF + hw])
                            return _DualInst(i1, i2)
                        mat_steps.append(("dma", SY, store, (("macc", li),),
                                          ((matname, ch, hh),), "sm"))
                fi = MAT_FLAG_IDX[(matname, hh)]
                def flag(e, fi=fi):
                    with e.If_eq(rme_s, 0):
                        i1 = e.dma_start(out=FLG[0:1, fi * 16:(fi + 1) * 16],
                                         in_=s_mag[:])
                    with e.Else():
                        i2 = e.dma_start(
                            out=FLG[0:1, 96 + fi * 16:96 + (fi + 1) * 16],
                            in_=s_mag[:])
                    return _DualInst(i1, i2)
                mat_steps.append(("dma", SY, flag, ("mag", "rme_s"),
                                  ((matname, "done", hh),), "sm"))

        mat_pos = [0]

        def mat_step(n=1):
            for _ in range(n):
                if mat_pos[0] >= len(mat_steps):
                    return
                kind, eng, fn, rd, wr, stream = mat_steps[mat_pos[0]]
                mat_pos[0] += 1
                if kind == "dma":
                    p.dma(eng, fn, reads=rd, writes=wr, stream=stream)
                else:
                    p.op(eng, fn, reads=rd, writes=wr)
                for k in wr:
                    if isinstance(k, tuple) and len(k) == 3 and \
                            k[0] in ("wgm", "wum", "wdm"):
                        mat_emitted.add(k)

        polled = set()

        def mat_require_mh(pairs):
            for m, hh in pairs:
                while (m, "done", hh) not in mat_emitted:
                    if mat_pos[0] >= len(mat_steps):
                        raise AssertionError(f"mat exhausted for {(m, hh)}")
                    mat_step(1)
            for m, hh in pairs:
                if (m, hh) in polled:
                    continue
                polled.add((m, hh))
                fi = MAT_FLAG_IDX[(m, hh)]

                def poll(e, fi=fi):
                    sl = s_flag[0:1, fi * 16:(fi + 1) * 16]
                    with e.If_eq(rme_g, 0):
                        e.dma_start(out=sl,
                                    in_=FLG[0:1, 96 + fi * 16:
                                            96 + (fi + 1) * 16]
                                    ).then_inc(p.sems["sp"], 16)
                    with e.Else():
                        e.dma_start(out=sl,
                                    in_=FLG[0:1, fi * 16:(fi + 1) * 16]
                                    ).then_inc(p.sems["sp"], 16)
                    e.reg_mov(p_pz, 1)
                    with e.While(p_pz):
                        e.tensor_copy(out=s_dly[:], in_=s_dly[:])
                        with e.If_eq(rme_g, 0):
                            e.dma_start(out=sl,
                                        in_=FLG[0:1, 96 + fi * 16:
                                                96 + (fi + 1) * 16]
                                        ).then_inc(p.sems["sp"], 16)
                        with e.Else():
                            e.dma_start(out=sl,
                                        in_=FLG[0:1, fi * 16:(fi + 1) * 16]
                                        ).then_inc(p.sems["sp"], 16)
                        e.tensor_copy(out=s_dly[:], in_=s_dly[:])
                        e.reg_load(p_pr, s_flag[0:1, fi * 16:fi * 16 + 1])
                        e.reg_alu(p_pz, p_pr, MAGICI, ALU.not_equal)
                    return e.engine_nop()
                p._sem("sp")
                p.op(G, poll, reads=("mag", "dly", "rme_g"),
                     writes=(("poll", m, hh), "flagbuf"))

        def mat_flush():
            while mat_pos[0] < len(mat_steps):
                mat_step(1)

        out_streams = ["so0", "so1", "so2", "so3"]
        out_rr = [0]

        def out_stream():
            s = out_streams[out_rr[0] % 4]
            out_rr[0] += 1
            return s

        LTT = min(TT, 256)
        NLT = TT // LTT

        def ln_tile(tok0, dst_off, tag, old_tag, s_xn, s_xblk):
            p.dma(SY, lambda e: e.dma_start(
                    out=s_xblk[:].rearrange("p (c n) -> p c n", c=C),
                    in_=XTr[:, :, tok0:tok0 + LTT]),
                  writes=("xblk",), stream="sx")
            for ch in range(C):
                xc = s_xblk[:, ch * LTT:(ch + 1) * LTT]
                p.op(V, lambda e, xc=xc: e.tensor_copy(out=s_xbf[:, 0:LTT],
                                                       in_=xc),
                     reads=("xblk",), writes=("xbf",))
                p.op(T, lambda e, ch=ch: e.matmul(
                        out=pb[4][0:1, 0:LTT], lhsT=s_onesk[:],
                        rhs=s_xbf[:, 0:LTT],
                        start=(ch == 0), stop=(ch == C - 1)),
                     reads=("xbf", "onesk"), writes=("p4",),
                     accum_key="p4", accum_continue=(ch != 0))
                p.op(S_, lambda e: e.activation(out=s_xsq[:, 0:LTT],
                                                in_=s_xbf[:, 0:LTT],
                                                func=AF.Square),
                     reads=("xbf",), writes=("xsq",))
                p.op(T, lambda e, ch=ch: e.matmul(
                        out=pb[5][0:1, 0:LTT], lhsT=s_onesk[:],
                        rhs=s_xsq[:, 0:LTT],
                        start=(ch == 0), stop=(ch == C - 1)),
                     reads=("xsq", "onesk"), writes=("p5",),
                     accum_key="p5", accum_continue=(ch != 0))
                mat_step(1)
            r_m = s_rows[0:1, 0:LTT]
            r_v = s_rows[0:1, TT:TT + LTT]
            r_r = s_rows[0:1, 2 * TT:2 * TT + LTT]
            r_mm = s_rows[0:1, 3 * TT:3 * TT + LTT]
            p.op(V, lambda e: e.tensor_scalar(
                    out=r_m, in0=pb[4][0:1, 0:LTT], scalar1=1.0 / D,
                    scalar2=None, op0=ALU.mult, op1=ALU.bypass),
                 reads=("p4",), writes=("r_m",))
            p.op(V, lambda e: e.tensor_scalar(
                    out=r_v, in0=pb[5][0:1, 0:LTT], scalar1=1.0 / D,
                    scalar2=None, op0=ALU.mult, op1=ALU.bypass),
                 reads=("p5",), writes=("r_v",))
            p.op(V, lambda e: e.tensor_tensor(out=r_mm, in0=r_m, in1=r_m,
                                              op=ALU.mult),
                 reads=("r_m",), writes=("r_mm", "qrow"))
            p.op(V, lambda e: e.scalar_tensor_tensor(
                    out=r_v, in0=r_v, scalar=1e-5, in1=r_mm,
                    op0=ALU.add, op1=ALU.subtract),
                 reads=("r_v", "r_mm"), writes=("r_v",))
            p.op(S_, lambda e: e.activation(out=r_r, in_=r_v, func=AF.Ln),
                 reads=("r_v",), writes=("r_r",))
            p.op(S_, lambda e: e.activation(out=r_r, in_=r_r, func=AF.Exp,
                                            scale=-0.5),
                 reads=("r_r",), writes=("r_r",))
            p.op(V, lambda e: e.tensor_scalar(
                    out=r_m, in0=r_m, scalar1=-1.0, scalar2=None,
                    op0=ALU.mult, op1=ALU.bypass),
                 reads=("r_m",), writes=("r_m",))
            p.op(T, lambda e: e.matmul(out=pb[4][:, 0:LTT], lhsT=s_ones1[:],
                                       rhs=r_m, start=True, stop=True),
                 reads=("r_m", "ones1"), writes=("p4",))
            p.op(V, lambda e: e.tensor_copy(out=s_mrep[:, 0:LTT],
                                            in_=pb[4][:, 0:LTT]),
                 reads=("p4",), writes=("mrep",))
            p.op(T, lambda e: e.matmul(out=pb[5][:, 0:LTT], lhsT=s_ones1[:],
                                       rhs=r_r, start=True, stop=True),
                 reads=("r_r", "ones1"), writes=("p5",))
            p.op(V, lambda e: e.tensor_copy(out=s_rrep[:, 0:LTT],
                                            in_=pb[5][:, 0:LTT]),
                 reads=("p5",), writes=("rrep",))
            for ch in range(C):
                xc = s_xblk[:, ch * LTT:(ch + 1) * LTT]
                p.op(V, lambda e, xc=xc: e.tensor_tensor(
                        out=xc, in0=xc, in1=s_mrep[:, 0:LTT], op=ALU.add),
                     reads=("xblk", "mrep"), writes=("xblk",))
                wkeys = (("xn", tag, ch),)
                if old_tag is not None:
                    wkeys = wkeys + (("xn", old_tag, ch),)
                p.op(V, lambda e, xc=xc, ch=ch: e.tensor_tensor(
                        out=s_xn[:, ch * TOK + dst_off:
                                 ch * TOK + dst_off + LTT],
                        in0=xc, in1=s_rrep[:, 0:LTT], op=ALU.mult),
                     reads=("xblk", "rrep"), writes=wkeys)
                mat_step(1)

        wq_bufs = [s_w0, s_w1]

        def load_wslab(WSRCr, col0, width, buf_i, tag):
            buf = wq_bufs[buf_i]
            dst = buf[:, 0:C * SLAB].rearrange("p (c n) -> p c n", c=C)
            new = tuple((tag, f) for f in range(C))
            old = cur_keys.get(f"w{buf_i}", ())
            cur_keys[f"w{buf_i}"] = new
            p.dma(G, lambda e, dst=dst, col0=col0, width=width, WSRCr=WSRCr:
                  e.dma_start(out=dst[:, :, 0:width],
                              in_=WSRCr[:, :, col0:col0 + width]),
                  writes=new + tuple(old), stream=f"sw{buf_i}")

        emit_mat("wgm", GIDXT, WGM, CL, CL * 128, I, halves=(0,))
        emit_mat("wum", UIDXT, WUM, CL, CL * 128, I, halves=(0,))
        emit_mat("wgm", GIDXT, WGM, CL, CL * 128, I, halves=(1,))
        emit_mat("wum", UIDXT, WUM, CL, CL * 128, I, halves=(1,))
        emit_mat("wdm", DIDXT, WDM, ICL, ICL * 128, D)

        stkA2 = ctx.enter_context(ExitStack())
        s_q = sb("s_q", [128, HGH * TOK], BF16, stkA2)
        s_kf = sb("s_kf", [128, HGH * TOK], BF16, stkA2)
        s_ktm = sb("s_ktm", [128, TB * GW], BF16, stkA2)
        s_vtm = sb("s_vtm", [128, TB * GW], BF16, stkA2)

        RCH = RB // NB
        for rc in range(RCH):
            old_tag = None if rc == 0 else ("r", rc - 1)
            for tt in range(NTT):
                for lt in range(NLT):
                    ln_tile(rc * TOK + tt * TT + lt * LTT,
                            tt * TT + lt * LTT, ("r", rc), old_tag,
                            s_xn, s_xblk)
            for hg in range(NHG):
                for kv in range(2):
                    col0 = (1 + kv) * D + hg * GW
                    bi = kv
                    load_wslab(WQKVTr, col0, GW, bi, ("wkv", rc, hg, kv))
                    dst = s_ktm if kv == 0 else s_vtm
                    nm = "rk" if kv == 0 else "rv"
                    for tb in range(TB):
                        pbi = tb % 2
                        for f in range(C):
                            p.op(T, lambda e, f=f, tb=tb, bi=bi, pbi=pbi,
                                 rc=rc: e.matmul(
                                    out=pb[pbi][:, 0:GW],
                                    lhsT=s_xn[:, f * TOK + tb * 128:
                                              f * TOK + (tb + 1) * 128],
                                    rhs=wq_bufs[bi][:, f * SLAB:
                                                    f * SLAB + GW],
                                    start=(f == 0), stop=(f == C - 1)),
                                 reads=((("wkv", rc, hg, bi), f),
                                        ("xn", ("r", rc), f)),
                                 writes=(f"p{pbi}",), accum_key=f"p{pbi}",
                                 accum_continue=(f != 0), inc=(f == C - 1))
                        chunks = list(range(HGH))
                        wkeys = tuple((nm, rc, hg, tb, hl) for hl in chunks)
                        old = rekey_regions("ktmbuf" if kv == 0 else "vtmbuf",
                                            tb, chunks, wkeys)
                        p.op(S_, lambda e, tb=tb, pbi=pbi, dst=dst:
                             e.activation(out=dst[:, tb * GW:(tb + 1) * GW],
                                          in_=pb[pbi][:, 0:GW],
                                          func=AF.Silu),
                             reads=(f"p{pbi}",), writes=wkeys + old)
                        mat_step(3)
                for nb_ in range(NB):
                    rb = rc * NB + nb_
                    p.op(V, lambda e, rb=rb: e.tensor_scalar(
                            out=s_frow[:], in0=s_bdm1[:],
                            scalar1=s_gates[0:1, rb:rb + 1],
                            scalar2=1.0, op0=ALU.mult, op1=ALU.add),
                         reads=("bdm1", "gates"), writes=("frow",))
                    p.op(T, lambda e: e.matmul(
                            out=pb[4][:, 0:H], lhsT=s_ones1[:], rhs=s_frow[:],
                            start=True, stop=True),
                         reads=("frow", "ones1"), writes=("p4",))
                    p.op(V, lambda e: e.tensor_copy(out=s_fbc[:],
                                                    in_=pb[4][:, 0:H]),
                         reads=("p4",), writes=("fbc",))
                    p.op(V, lambda e, rb=rb: e.tensor_scalar(
                            out=s_gkdec[:], in0=s_kdec[:],
                            scalar1=s_gB[:, rb:rb + 1], scalar2=None,
                            op0=ALU.mult, op1=ALU.bypass),
                         reads=tuple(("kdec", h, hf) for h in range(H)
                                     for hf in range(KH)) + ("gB",),
                         writes=("gkdec",))
                    for hl in range(HGH):
                        h = hg * HGH + hl
                        for half in range(KH):
                            tb = nb_ * KH + half
                            p.op(V, lambda e, tb=tb, h=h, hl=hl, half=half,
                                 rc=rc, hg=hg: e.tensor_scalar(
                                    out=s_kd[:],
                                    in0=s_ktm[:, tb * GW + hl * 128:
                                              tb * GW + (hl + 1) * 128],
                                    scalar1=s_gkdec[:, half * H + h:
                                                    half * H + h + 1],
                                    scalar2=None, op0=ALU.mult,
                                    op1=ALU.bypass),
                                 reads=(("rk", rc, hg, tb, hl), "gkdec"),
                                 writes=("kd",))
                            p.op(T, lambda e, tb=tb, hl=hl, half=half,
                                 rc=rc, hg=hg: e.matmul(
                                    out=pb[2][:, 0:128], lhsT=s_kd[:],
                                    rhs=s_vtm[:, tb * GW + hl * 128:
                                              tb * GW + (hl + 1) * 128],
                                    start=(half == 0),
                                    stop=(half == KH - 1)),
                                 reads=("kd", ("rv", rc, hg, tb, hl)),
                                 writes=("p2",), accum_key="p2",
                                 accum_continue=(half != 0))
                        p.op(V, lambda e, h=h: e.scalar_tensor_tensor(
                                out=s_S[:, h * 128:(h + 1) * 128],
                                in0=s_S[:, h * 128:(h + 1) * 128],
                                scalar=s_fbc[:, h:h + 1],
                                in1=pb[2][:, 0:128], op0=ALU.mult,
                                op1=ALU.add),
                             reads=(("S", h), "fbc", "p2"),
                             writes=(("S", h),))
                        mat_step(1)

        for tt in range(NTT):
            for lt in range(NLT):
                ln_tile(3 * TOK + tt * TT + lt * LTT, tt * TT + lt * LTT,
                        "own", ("r", RCH - 1) if RCH > 0 else None,
                        s_xn, s_xblk)

        for sl in range(D // SLAB):
            bi = sl % 2
            load_wslab(WGTr, sl * SLAB, SLAB, bi, ("gatew", sl))
            for th in range(NTT):
                for oc in range(SLAB // 128):
                    o = sl * SLAB + oc * 128
                    pbi = oc % 2
                    for f in range(C):
                        p.op(T, lambda e, f=f, th=th, oc=oc, bi=bi, pbi=pbi:
                             e.matmul(
                                out=pb[pbi][:, 0:TT],
                                lhsT=wq_bufs[bi][:, f * SLAB + oc * 128:
                                                 f * SLAB + (oc + 1) * 128],
                                rhs=s_xn[:, f * TOK + th * TT:
                                         f * TOK + th * TT + TT],
                                start=(f == 0), stop=(f == C - 1)),
                             reads=((("gatew", sl), f), ("xn", "own", f)),
                             writes=(f"p{pbi}",), accum_key=f"p{pbi}",
                             accum_continue=(f != 0), inc=(f == C - 1))
                    ch = o // 128
                    p.op(S_, lambda e, pbi=pbi: e.activation(
                            out=s_gtile[:], in_=pb[pbi][:, 0:TT],
                            func=AF.Sigmoid),
                         reads=(f"p{pbi}",), writes=("gtile",))
                    p.dma(SY, lambda e, ch=ch, th=th: e.dma_start(
                            out=GATS[ch * 128:(ch + 1) * 128,
                                     th * TT:th * TT + TT],
                            in_=s_gtile[:]),
                          reads=("gtile",), writes=(("gats", ch, th),),
                          stream="sst")
                    mat_step(3)

        p.dma(SY, lambda e: e.dma_start(
                out=XNSr[:, :, :],
                in_=s_xn[:].rearrange("p (c n) -> p c n", c=C)),
              reads=tuple(("xn", "own", ch) for ch in range(C)),
              writes=("xns",), stream="sst")

        for hg in range(NHG):
            for qk, dstbuf, dsttag in ((0, s_q, "q"), (1, s_kf, "kf")):
                col0 = qk * D + hg * GW
                bi = qk
                load_wslab(WQKVTr, col0, GW, bi, (dsttag, "w", hg))
                for th in range(NTT):
                    for oc in range(GW // 128):
                        pbi = oc % 2
                        for f in range(C):
                            p.op(T, lambda e, f=f, th=th, oc=oc, bi=bi,
                                 pbi=pbi: e.matmul(
                                    out=pb[pbi][:, 0:TT],
                                    lhsT=wq_bufs[bi][:, f * SLAB + oc * 128:
                                                     f * SLAB + (oc+1)*128],
                                    rhs=s_xn[:, f * TOK + th * TT:
                                             f * TOK + th * TT + TT],
                                    start=(f == 0), stop=(f == C - 1)),
                                 reads=(((dsttag, "w", hg), f),
                                        ("xn", "own", f)),
                                 writes=(f"p{pbi}",), accum_key=f"p{pbi}",
                                 accum_continue=(f != 0), inc=(f == C - 1))
                        wk = (dsttag, hg, oc, th)
                        rid = (dsttag + "buf", oc, th)
                        old = cur_keys.get(rid, ())
                        cur_keys[rid] = (wk,)
                        p.op(S_, lambda e, oc=oc, th=th, pbi=pbi,
                             dstbuf=dstbuf: e.activation(
                                out=dstbuf[:, oc * TOK + th * TT:
                                           oc * TOK + th * TT + TT],
                                in_=pb[pbi][:, 0:TT], func=AF.Silu),
                             reads=(f"p{pbi}",),
                             writes=(wk,) + tuple(old))
                        mat_step(3)
            load_wslab(WQKVTr, 2 * D + hg * GW, GW, 0, ("vw", hg))
            for tb in range(TB):
                pbi = tb % 2
                for f in range(C):
                    p.op(T, lambda e, f=f, tb=tb, pbi=pbi: e.matmul(
                            out=pb[pbi][:, 0:GW],
                            lhsT=s_xn[:, f * TOK + tb * 128:
                                      f * TOK + (tb + 1) * 128],
                            rhs=wq_bufs[0][:, f * SLAB:f * SLAB + GW],
                            start=(f == 0), stop=(f == C - 1)),
                         reads=((("vw", hg), f), ("xn", "own", f)),
                         writes=(f"p{pbi}",), accum_key=f"p{pbi}",
                         accum_continue=(f != 0), inc=(f == C - 1))
                chunks = list(range(HGH))
                wkeys = tuple(("vtm", hg, tb, hl) for hl in chunks)
                old = rekey_regions("vtmbuf", tb, chunks, wkeys)
                p.op(S_, lambda e, tb=tb, pbi=pbi: e.activation(
                        out=s_vtm[:, tb * GW:(tb + 1) * GW],
                        in_=pb[pbi][:, 0:GW], func=AF.Silu),
                     reads=(f"p{pbi}",), writes=wkeys + old)
                mat_step(3)
            for tb in range(TB):
                for hl in range(HGH):
                    old = rekey_regions("ktmbuf", tb, [hl],
                                        [("ktmo", hg, tb, hl)])
                    p.op(T, lambda e, tb=tb, hl=hl: e.transpose(
                            out=pT[:, 0:128],
                            in_=s_kf[:, hl * TOK + tb * 128:
                                     hl * TOK + (tb + 1) * 128],
                            identity=s_id[:]),
                         reads=(("kf", hg, hl, (tb * 128) // TT), "id"),
                         writes=("pT",))
                    p.op(V, lambda e, tb=tb, hl=hl: e.tensor_copy(
                            out=s_ktm[:, tb * GW + hl * 128:
                                      tb * GW + (hl + 1) * 128],
                            in_=pT[:, 0:128]),
                         reads=("pT",), writes=(("ktmo", hg, tb, hl),) +
                         tuple(old))
                    mat_step(1)
            for hl in range(HGH):
                h = hg * HGH + hl
                for blk in range(NB):
                    t0 = blk * B
                    th_of = t0 // TT
                    p.op(V, lambda e, hl=hl, h=h, t0=t0: e.tensor_tensor(
                            out=s_qd[:],
                            in0=s_q[:, hl * TOK + t0:hl * TOK + t0 + B],
                            in1=s_qdecR[:, h * B:(h + 1) * B], op=ALU.mult),
                         reads=(("q", hg, hl, th_of), ("qdecR", h)),
                         writes=("qdt",))
                    p.op(V, lambda e, h=h: e.tensor_copy(
                            out=s_Sbf[:, h * 128:(h + 1) * 128],
                            in_=s_S[:, h * 128:(h + 1) * 128]),
                         reads=(("S", h),), writes=(("Sbf", h),))
                    for half in range(KH):
                        p.op(T, lambda e, hl=hl, t0=t0, half=half: e.matmul(
                                out=pb[half][:, 0:B],
                                lhsT=s_kf[:, hl * TOK + t0 + half * 128:
                                          hl * TOK + t0 + half * 128 + 128],
                                rhs=s_q[:, hl * TOK + t0:hl * TOK + t0 + B],
                                start=True, stop=True),
                             reads=(("kf", hg, hl, th_of),
                                    ("q", hg, hl, th_of)),
                             writes=(f"p{half}",))
                        p.op(V, lambda e, h=h, half=half: e.tensor_tensor(
                                out=s_sT[:, half * B:(half + 1) * B],
                                in0=pb[half][:, 0:B],
                                in1=s_diag[:, (half * H + h) * B:
                                           (half * H + h + 1) * B],
                                op=ALU.mult),
                             reads=(f"p{half}", ("diag", h, half)),
                             writes=(("sT", half),))
                    p.op(T, lambda e, h=h: e.matmul(
                            out=pb[2][:, 0:B],
                            lhsT=s_Sbf[:, h * 128:(h + 1) * 128],
                            rhs=s_qd[:], start=True, stop=False),
                         reads=(("Sbf", h), "qdt"), writes=("p2",),
                         accum_key="p2", inc=False)
                    for half in range(KH):
                        tb = blk * KH + half
                        p.op(T, lambda e, hl=hl, tb=tb, half=half, hg=hg:
                             e.matmul(
                                out=pb[2][:, 0:B],
                                lhsT=s_vtm[:, tb * GW + hl * 128:
                                           tb * GW + (hl + 1) * 128],
                                rhs=s_sT[:, half * B:(half + 1) * B],
                                start=False, stop=(half == KH - 1)),
                             reads=(("vtm", hg, tb, hl), ("sT", half)),
                             writes=("p2",), accum_key="p2",
                             accum_continue=True, inc=(half == KH - 1))
                    p.op(V, lambda e: e.tensor_copy(out=s_atile[:],
                                                    in_=pb[2][:, 0:B]),
                         reads=("p2",), writes=("atile",))
                    p.dma(SY, lambda e, h=h, t0=t0: e.dma_start(
                            out=ATTS[h * 128:(h + 1) * 128, t0:t0 + B],
                            in_=s_atile[:]),
                          reads=("atile",), writes=(("atts", h, blk),),
                          stream="sat")
                    for half in range(KH):
                        tb = blk * KH + half
                        p.op(V, lambda e, h=h, hl=hl, tb=tb, half=half,
                             hg=hg: e.tensor_scalar(
                                out=s_kd[:],
                                in0=s_ktm[:, tb * GW + hl * 128:
                                          tb * GW + (hl + 1) * 128],
                                scalar1=s_kdec[:, half * H + h:
                                               half * H + h + 1],
                                scalar2=None, op0=ALU.mult, op1=ALU.bypass),
                             reads=(("ktmo", hg, tb, hl), ("kdec", h, half)),
                             writes=("kd",))
                        p.op(T, lambda e, hl=hl, tb=tb, half=half, hg=hg:
                             e.matmul(
                                out=pb[3][:, 0:128], lhsT=s_kd[:],
                                rhs=s_vtm[:, tb * GW + hl * 128:
                                          tb * GW + (hl + 1) * 128],
                                start=(half == 0), stop=(half == KH - 1)),
                             reads=("kd", ("vtm", hg, tb, hl)),
                             writes=("p3",), accum_key="p3",
                             accum_continue=(half != 0))
                    p.op(V, lambda e, h=h: e.scalar_tensor_tensor(
                            out=s_S[:, h * 128:(h + 1) * 128],
                            in0=s_S[:, h * 128:(h + 1) * 128],
                            scalar=s_bdB[:, h:h + 1], in1=pb[3][:, 0:128],
                            op0=ALU.mult, op1=ALU.add),
                         reads=(("S", h), "bdB", "p3"), writes=(("S", h),))
                    mat_step(4)

        stkA2.close()
        phase_barrier()

        stkA3 = ctx.enter_context(ExitStack())
        s_ng = sb("s_ng", [128, C * TOK], BF16, stkA3)
        for th in range(NTT):
            p.dma(SY, lambda e, th=th: e.dma_start(
                    out=s_ng[:].rearrange("p (c n) -> p c n", c=C)
                    [:, :, th * TT:th * TT + TT],
                    in_=ATTSr[:, :, th * TT:th * TT + TT]),
                  reads=tuple(("atts", h, blk) for h in range(H)
                              for blk in range(NB)),
                  writes=tuple(("attn", ch, th) for ch in range(C)),
                  stream="sar")
            for ch in range(C):
                p.op(S_, lambda e, ch=ch, th=th: e.activation(
                        out=s_xsq[:, 0:TT],
                        in_=s_ng[:, ch * TOK + th * TT:
                                 ch * TOK + th * TT + TT],
                        func=AF.Square),
                     reads=(("attn", ch, th),), writes=("xsq",))
                p.op(T, lambda e, ch=ch: e.matmul(
                        out=pb[5][0:1, 0:TT], lhsT=s_onesk[:],
                        rhs=s_xsq[:, 0:TT],
                        start=(ch == 0), stop=(ch == C - 1)),
                     reads=("xsq", "onesk"), writes=("p5",),
                     accum_key="p5", accum_continue=(ch != 0))
                mat_step(1)
            r_v = s_rows[0:1, TT:2 * TT]
            r_r = s_rows[0:1, 2 * TT:3 * TT]
            p.op(V, lambda e: e.tensor_scalar(
                    out=r_v, in0=pb[5][0:1, 0:TT], scalar1=1.0 / D,
                    scalar2=1e-6, op0=ALU.mult, op1=ALU.add),
                 reads=("p5",), writes=("r_v",))
            p.op(S_, lambda e: e.activation(out=r_r, in_=r_v, func=AF.Ln),
                 reads=("r_v",), writes=("r_r",))
            p.op(S_, lambda e: e.activation(out=r_r, in_=r_r, func=AF.Exp,
                                            scale=-0.5),
                 reads=("r_r",), writes=("r_r",))
            p.op(T, lambda e: e.matmul(out=pb[5][:, 0:TT], lhsT=s_ones1[:],
                                       rhs=r_r, start=True, stop=True),
                 reads=("r_r", "ones1"), writes=("p5",))
            p.op(V, lambda e: e.tensor_copy(out=s_rrep[:, 0:TT],
                                            in_=pb[5][:, 0:TT]),
                 reads=("p5",), writes=("rrep",))
            for ch in range(C):
                sl_ = slice(ch * TOK + th * TT, ch * TOK + th * TT + TT)
                p.op(V, lambda e, sl_=sl_: e.tensor_tensor(
                        out=s_ngt[:], in0=s_ng[:, sl_], in1=s_rrep[:, 0:TT],
                        op=ALU.mult),
                     reads=(("attn", ch, th), "rrep"), writes=("ngt",))
                p.op(V, lambda e, ch=ch: e.tensor_scalar(
                        out=s_ngt[:], in0=s_ngt[:],
                        scalar1=s_normw[:, ch:ch + 1], scalar2=None,
                        op0=ALU.mult, op1=ALU.bypass),
                     reads=("ngt", "normw"), writes=("ngt",))
                p.dma(SY, lambda e, ch=ch, th=th: e.dma_start(
                        out=s_gtile[:],
                        in_=GATS[ch * 128:(ch + 1) * 128,
                                 th * TT:th * TT + TT]),
                      reads=(("gats", ch, th),), writes=("gtile",),
                      stream="sar")
                p.op(V, lambda e, sl_=sl_: e.tensor_tensor(
                        out=s_ng[:, sl_], in0=s_ngt[:], in1=s_gtile[:],
                        op=ALU.mult),
                     reads=("ngt", "gtile"),
                     writes=(("ng", ch, th), ("attn", ch, th)))
                mat_step(2)

        for sl in range(D // SLAB):
            bi = sl % 2
            load_wslab(WOTr, sl * SLAB, SLAB, bi, ("wow", sl))
            for th in range(NTT):
                for oc in range(SLAB // 128):
                    o = sl * SLAB + oc * 128
                    pbi = oc % 2
                    for f in range(C):
                        p.op(T, lambda e, f=f, th=th, oc=oc, bi=bi, pbi=pbi:
                             e.matmul(
                                out=pb[pbi][:, 0:TT],
                                lhsT=wq_bufs[bi][:, f * SLAB + oc * 128:
                                                 f * SLAB + (oc + 1) * 128],
                                rhs=s_ng[:, f * TOK + th * TT:
                                         f * TOK + th * TT + TT],
                                start=(f == 0), stop=(f == C - 1)),
                             reads=((("wow", sl), f), ("ng", f, th)),
                             writes=(f"p{pbi}",), accum_key=f"p{pbi}",
                             accum_continue=(f != 0), inc=(f == C - 1))
                    ch = o // 128
                    p.dma(SY, lambda e, ch=ch, th=th: e.dma_start(
                            out=s_acc[:],
                            in_=XTr[:, ch, 3 * TOK + th * TT:
                                    3 * TOK + th * TT + TT]),
                          writes=("acc",), stream="sx")
                    p.op(V, lambda e, pbi=pbi: e.tensor_tensor(
                            out=s_acc[:], in0=s_acc[:], in1=pb[pbi][:, 0:TT],
                            op=ALU.add),
                         reads=("acc", f"p{pbi}"), writes=("acc",))
                    p.dma(SY, lambda e, ch=ch, th=th: e.dma_start(
                            out=ACCS[ch * 128:(ch + 1) * 128,
                                     th * TT:th * TT + TT],
                            in_=s_acc[:]),
                          reads=("acc",), writes=(("accs", ch, th),),
                          stream="sst")
                    mat_step(3)

        stkA3.close()
        stkA.close()
        phase_barrier()

        stkB = ctx.enter_context(ExitStack())
        s_xn2 = sb("s_xn2", [128, C * TOK], BF16, stkB)
        s_hf = sb("s_hf", [128, IC * TT], BF16, stkB)

        p.dma(SY, lambda e: e.dma_start(
                out=s_xn2[:].rearrange("p (c n) -> p c n", c=C),
                in_=XNSr[:, :, :]),
              reads=("xns",),
              writes=tuple(("xn2", ch) for ch in range(C)),
              stream="sx")


        def load_islab(WSRCr, col0, width, buf, tag, stream, matkey, bufnm,
                       extra_old=()):
            dst = buf[:, 0:C * SLAB].rearrange("p (c n) -> p c n", c=C)
            new = tuple((tag, f) for f in range(C))
            old = cur_keys.get(bufnm, ())
            cur_keys[bufnm] = new
            rd_h = set()
            if col0 < I // 2:
                rd_h.add(0)
            if col0 + width > I // 2:
                rd_h.add(1)
            rds = tuple((matkey, "done", hh) for hh in rd_h) + \
                tuple(("poll", matkey, hh) for hh in rd_h)
            p.dma(G, lambda e, dst=dst, col0=col0, width=width, WSRCr=WSRCr:
                  e.dma_start(out=dst[:, :, 0:width],
                              in_=WSRCr[:, :, col0:col0 + width]),
                  reads=rds, writes=new + tuple(old) + tuple(extra_old),
                  stream=stream)

        gslabs = []
        icol = 0
        while icol < I:
            w = min(SLAB, I - icol)
            gslabs.append((icol, w))
            icol += w
        alias_pending = ()
        for th in range(NTT):
            stkU = ctx.enter_context(ExitStack())
            s_u0 = sb(f"s_u0_{th}", [128, C * SLAB], BF16, stkU)
            s_u1 = sb(f"s_u1_{th}", [128, C * SLAB], BF16, stkU)
            uslab = [s_u0, s_u1]
            cur_keys[f"u0"] = cur_keys.get("u0", ())
            for si, (col0, w) in enumerate(gslabs):
                bi = si % 2
                halves = set()
                if col0 < I // 2:
                    halves.add(0)
                if col0 + w > I // 2:
                    halves.add(1)
                mat_require_mh([(m, hh) for m in ("wgm", "wum")
                                for hh in halves])
                load_islab(WGMr, col0, w, wq_bufs[bi], ("gwf", th, si),
                           f"sw{bi}", "wgm", f"w{bi}")
                load_islab(WUMr, col0, w, uslab[bi], ("uwf", th, si),
                           f"su{bi}", "wum", f"u{bi}_{th}",
                           extra_old=(alias_pending if si < 2 else ()))
                for oc in range(w // 128):
                    ic = (col0 + oc * 128) // 128
                    for f in range(C):
                        p.op(T, lambda e, f=f, th=th, oc=oc, bi=bi, si=si:
                             e.matmul(
                                out=pb[0][:, 0:TT],
                                lhsT=wq_bufs[bi][:, f * SLAB + oc * 128:
                                                 f * SLAB + (oc + 1) * 128],
                                rhs=s_xn2[:, f * TOK + th * TT:
                                          f * TOK + th * TT + TT],
                                start=(f == 0), stop=(f == C - 1)),
                             reads=((("gwf", th, si), f), ("xn2", f)),
                             writes=("p0",), accum_key="p0",
                             accum_continue=(f != 0), inc=(f == C - 1))
                    p.op(S_, lambda e: e.activation(
                            out=s_sg[:], in_=pb[0][:, 0:TT], func=AF.Silu),
                         reads=("p0",), writes=("sg",))
                    for f in range(C):
                        p.op(T, lambda e, f=f, th=th, oc=oc, bi=bi, si=si:
                             e.matmul(
                                out=pb[1][:, 0:TT],
                                lhsT=uslab[bi][:, f * SLAB + oc * 128:
                                               f * SLAB + (oc + 1) * 128],
                                rhs=s_xn2[:, f * TOK + th * TT:
                                          f * TOK + th * TT + TT],
                                start=(f == 0), stop=(f == C - 1)),
                             reads=((("uwf", th, si), f), ("xn2", f)),
                             writes=("p1",), accum_key="p1",
                             accum_continue=(f != 0), inc=(f == C - 1))
                    p.op(V, lambda e, ic=ic: e.tensor_tensor(
                            out=s_hf[:, ic * TT:(ic + 1) * TT],
                            in0=pb[1][:, 0:TT], in1=s_sg[:], op=ALU.mult),
                         reads=("p1", "sg"), writes=(("hf", ic),))
                    mat_step(8)
            alias_pending = tuple(cur_keys.get(f"u0_{th}", ())) + \
                tuple(cur_keys.get(f"u1_{th}", ()))
            stkU.close()
            stkD = ctx.enter_context(ExitStack())
            s_dw0 = sb(f"s_dw0_{th}", [128, IC * 128], BF16, stkD)
            s_dw1 = sb(f"s_dw1_{th}", [128, IC * 128], BF16, stkD)
            dwb = [s_dw0, s_dw1]
            for oc in range(C):
                bi = oc % 2
                dhalves = {0} if (oc + 1) * 128 <= D // 2 else (
                    {1} if oc * 128 >= D // 2 else {0, 1})
                mat_require_mh([("wdm", hh) for hh in dhalves])
                dst = dwb[bi][:, 0:IC * 128].rearrange("p (c n) -> p c n",
                                                       c=IC)
                new = tuple(("dw", th, oc, f) for f in range(IC))
                old = cur_keys.get(f"dw{bi}_{th}", ())
                cur_keys[f"dw{bi}_{th}"] = new
                p.dma(G, lambda e, dst=dst, oc=oc: e.dma_start(
                        out=dst[:, :, :],
                        in_=WDMr[:, 0:IC, oc * 128:(oc + 1) * 128]),
                      reads=tuple(("wdm", "done", hh) for hh in dhalves) +
                      tuple(("poll", "wdm", hh) for hh in dhalves),
                      writes=new + tuple(old) +
                      (alias_pending if oc < 2 else ()),
                      stream=f"sd{bi}")
                for f in range(IC):
                    p.op(T, lambda e, f=f, th=th, oc=oc, bi=bi: e.matmul(
                            out=pb[2][:, 0:TT],
                            lhsT=dwb[bi][:, f * 128:(f + 1) * 128],
                            rhs=s_hf[:, f * TT:(f + 1) * TT],
                            start=(f == 0), stop=(f == IC - 1)),
                         reads=(("dw", th, oc, f), ("hf", f)),
                         writes=("p2",), accum_key="p2",
                         accum_continue=(f != 0), inc=(f == IC - 1))
                p.dma(SY, lambda e, oc=oc, th=th: e.dma_start(
                        out=s_acc[:],
                        in_=ACCS[oc * 128:(oc + 1) * 128,
                                 th * TT:th * TT + TT]),
                      reads=(("accs", oc, th),), writes=("acc",),
                      stream="sx")
                p.op(V, lambda e: e.tensor_tensor(
                        out=s_acc[:], in0=s_acc[:], in1=pb[2][:, 0:TT],
                        op=ALU.add),
                     reads=("acc", "p2"), writes=("acc",))
                mat_step(10)
                p.dma(SY, lambda e, oc=oc, th=th: e.dma_start(
                        out=OUTT[oc * 128:(oc + 1) * 128,
                                 th * TT:th * TT + TT],
                        in_=s_acc[:]),
                      reads=("acc",), writes=(), stream=out_stream())
            alias_pending = tuple(cur_keys.get(f"dw0_{th}", ())) + \
                tuple(cur_keys.get(f"dw1_{th}", ()))
            stkD.close()

        p.wait_all(SY, out_streams)
        p.finalize()
        nc._prog_counts = dict(p.count)

    return nc


def _run(cfg, x, slope, w_qkv, w_gate, w_out, norm_w, lut, gidx, uidx, didx,
         runner=run_bass_kernel_spmd):
    x = np.asarray(x)
    slope = np.asarray(slope)
    lut = np.asarray(lut)
    L0_attn_w_qkv, L0_attn_w_gate, L0_attn_w_out = w_qkv, w_gate, w_out
    L0_attn_norm_w = norm_w
    L0_ffn_gate_idx, L0_ffn_up_idx, L0_ffn_down_idx = gidx, uidx, didx

    nc = build(cfg, [float(v) for v in lut], [float(v) for v in slope],
               race_check=bool(os.environ.get("KMK_RACE")))

    wqkvT = np.ascontiguousarray(np.asarray(L0_attn_w_qkv).T)
    wgT = np.ascontiguousarray(np.asarray(L0_attn_w_gate).T)
    woT = np.ascontiguousarray(np.asarray(L0_attn_w_out).T)
    normwT = np.ascontiguousarray(
        np.asarray(L0_attn_norm_w).reshape(cfg.C, 128).T)
    gidxT = np.ascontiguousarray(np.asarray(L0_ffn_gate_idx).T)
    uidxT = np.ascontiguousarray(np.asarray(L0_ffn_up_idx).T)
    didxT = np.ascontiguousarray(np.asarray(L0_ffn_down_idx).T)
    CL = cfg.C // 2
    ICL = (cfg.IC + 1) // 2
    didxT_pad = np.zeros((2 * ICL * 128, cfg.D), dtype=np.int32)
    didxT_pad[:cfg.I, :] = didxT
    gidx_half = [np.ascontiguousarray(gidxT[m * CL * 128:(m + 1) * CL * 128])
                 for m in range(2)]
    uidx_half = [np.ascontiguousarray(uidxT[m * CL * 128:(m + 1) * CL * 128])
                 for m in range(2)]
    didx_half = [np.ascontiguousarray(
        didxT_pad[m * ICL * 128:(m + 1) * ICL * 128]) for m in range(2)]
    slope_row = np.ascontiguousarray(slope.reshape(1, cfg.H))
    xT = [np.ascontiguousarray(x[s].T) for s in range(2)]

    in_maps = []
    for core in range(NCORES):
        seq, q = core % 2, core // 2
        gates = np.zeros((1, max(cfg.RB, 2)), dtype=np.float32)
        gates[0, :q * cfg.NB] = 1.0
        xt = np.empty((cfg.D, cfg.NSEQ), dtype=np.float32)
        xt[:, 3 * cfg.TOK:] = xT[seq][:, q * cfg.TOK:(q + 1) * cfg.TOK]
        xt[:, :3 * cfg.TOK] = xT[seq][:, :3 * cfg.TOK]
        me = core % 2
        in_maps.append({
            "XT": xt, "WQKVT": wqkvT, "WGT": wgT, "WOT": woT,
            "NORMWT": normwT, "SLOPE": slope_row,
            "GIDXT": gidx_half[me], "UIDXT": uidx_half[me],
            "DIDXT": didx_half[me], "GATES": gates,
            "MEIN": np.array([[me, 1 - me]], dtype=np.int32),
        })

    res = runner(nc, in_maps, core_ids=list(range(NCORES)))

    out = np.empty((2, 4 * cfg.TOK, cfg.D), dtype=np.float32)
    for core in range(NCORES):
        seq, q = core % 2, core // 2
        outT = np.asarray(res.results[core]["OUTT"])
        out[seq, q * cfg.TOK:(q + 1) * cfg.TOK, :] = outT.T
    return out


def kernel(L0_attn_hidden_states, L0_attn_slope_rate, L0_attn_w_qkv,
           L0_attn_w_gate, L0_attn_w_out, L0_attn_norm_w, L0_ffn_lut,
           L0_ffn_gate_idx, L0_ffn_up_idx, L0_ffn_down_idx):
    return _run(CFG_FULL, L0_attn_hidden_states, L0_attn_slope_rate,
                L0_attn_w_qkv, L0_attn_w_gate, L0_attn_w_out, L0_attn_norm_w,
                L0_ffn_lut, L0_ffn_gate_idx, L0_ffn_up_idx, L0_ffn_down_idx)
